# version 05
# baseline (speedup 1.0000x reference)
import sys
for _p in ("/opt/trn_rl_repo",):
    if _p not in sys.path:
        sys.path.insert(0, _p)

import numpy as np
import ml_dtypes
import concourse.bass as bass
import concourse.mybir as mybir
from concourse.bass_utils import run_bass_kernel_spmd

F32 = mybir.dt.float32
BF16 = mybir.dt.bfloat16
AF = mybir.ActivationFunctionType
ALU = mybir.AluOpType
AX = mybir.AxisListType.X

NEG_BIG = -1.0e30
INV_SQRT_DK = 1.0 / (128.0 ** 0.5)
LN_EPS = 1.0e-5
N_CB = 3
N_RB = 2


def banks_of(c0, c1):
    return list(range(c0 // 512, (c1 + 511) // 512))


class Cfg:
    def __init__(self, D=2048, S=2048, B=2, E=32, FE=768, TOPK=6, FSH=1536,
                 dbg=False, stop="full", nheads=None):
        self.D, self.S, self.B, self.E = D, S, B, E
        self.FE, self.TOPK, self.FSH = FE, TOPK, FSH
        self.NK = D // 128
        self.H = D // 128
        self.SEG = 512
        self.NJ = self.SEG // 128
        self.T = 1024
        self.NTC = self.T // 128
        self.NGRP = S // self.T
        self.FEC = FE // 128
        self.FSHC = FSH // 128
        self.NDQ = max(1, D // 512)
        self.dbg = dbg
        self.stop = stop
        self.nheads = nheads
        self.nexp = None
        assert self.FSHC % 2 == 0 and S % self.T == 0 and D % 512 == 0


class Sem:
    __slots__ = ("h", "c", "name")

    def __init__(self, nc, name):
        self.h = nc.alloc_semaphore(name)
        self.c = 0
        self.name = name


class Builder:
    def __init__(self, cfg: Cfg):
        self.cfg = cfg
        self.nc = bass.Bass("TRN2", target_bir_lowering=False,
                    dynamic_dma_scratch_size=16384)
        self._memo = {}
        self.bank_cons = [dict() for _ in range(8)]
        self.dbg_outs = set()

    def sem(self, name):
        return Sem(self.nc, name)

    def inc(self, ins, sem, n=1):
        ins.then_inc(sem.h, n)
        sem.c += n
        return sem.c

    def wait(self, eng, sem, tok):
        if sem is None or tok is None or tok <= 0:
            return
        if sem is self.s_xq or sem is self.s_st:
            tok = sem.c
        assert tok <= sem.c, f"wait on future token {sem.name}:{tok}>{sem.c}"
        key = (id(eng), sem.name)
        if self._memo.get(key, 0) >= tok:
            return
        eng.wait_ge(sem.h, tok)
        self._memo[key] = tok

    def bank_read(self, banks, sem, tok):
        for b in banks:
            d = self.bank_cons[b]
            if d.get(sem, 0) < tok:
                d[sem] = tok

    def pe_claim(self, banks):
        for b in banks:
            d = self.bank_cons[b]
            for sem, tok in d.items():
                self.wait(self.nc.tensor, sem, tok)
            d.clear()

    def declare_io(self):
        cfg, nc = self.cfg, self.nc
        D, NK, H, E = cfg.D, cfg.NK, cfg.H, cfg.E
        dt = nc.dram_tensor
        self.x_in = dt("x", [cfg.B, cfg.S, D], F32, kind="ExternalInput").ap()
        self.wq_r = dt("wq_r", [H, 128, NK * 128], F32, kind="ExternalInput").ap()
        self.wk_r = dt("wk_r", [H, 128, NK * 128], F32, kind="ExternalInput").ap()
        self.wv_r = dt("wv_r", [H, 128, NK * 128], F32, kind="ExternalInput").ap()
        self.wo_r = dt("wo_r", [NK, 128, NK * 128], F32, kind="ExternalInput").ap()
        self.betas_r = dt("betas_r", [128, H], F32, kind="ExternalInput").ap()
        self.gate_r = dt("gate_r", [128, NK * E], F32, kind="ExternalInput").ap()
        self.wg_r = dt("wg_r", [E, cfg.FEC, 128, NK * 128], F32,
                       kind="ExternalInput").ap()
        self.wu_r = dt("wu_r", [E, cfg.FEC, 128, NK * 128], F32,
                       kind="ExternalInput").ap()
        self.wd_r = dt("wd_r", [E, cfg.NDQ, 128, cfg.FEC * 512], F32,
                       kind="ExternalInput").ap()
        self.sg_r = dt("sg_r", [cfg.FSHC, 128, NK * 128], F32,
                       kind="ExternalInput").ap()
        self.su_r = dt("su_r", [cfg.FSHC, 128, NK * 128], F32,
                       kind="ExternalInput").ap()
        self.sd_r = dt("sd_r", [cfg.NDQ, 128, cfg.FSHC * 512], F32,
                       kind="ExternalInput").ap()
        self.y_out = dt("y", [cfg.B, cfg.S, D], F32, kind="ExternalOutput").ap()

        ident_np = np.eye(128, dtype=ml_dtypes.bfloat16)
        self.ident_d = nc.inline_tensor(ident_np, name="c_ident")
        NJ = cfg.NJ
        mw = 512 + (NJ - 1) * 128
        p = np.arange(128)[:, None]
        c = np.arange(mw)[None, :]
        mask_np = np.where(c <= (NJ - 1) * 128 + p, 1.0, 0.0)
        self.mask_d = nc.inline_tensor(
            mask_np.astype(ml_dtypes.bfloat16), name="c_mask")

    def declare_mem(self):
        cfg, nc = self.cfg, self.nc
        D, NK, H, T, NTC = cfg.D, cfg.NK, cfg.H, cfg.T, cfg.NTC
        sb = nc.alloc_sbuf_tensor
        self.beta_sb = sb("beta_sb", [128, H], F32)
        self.mem_sb = sb("mem_sb", [128, H * 128], F32)
        self.z_sb = sb("z_sb", [128, H], F32)
        self.z_bf = sb("z_bf", [128, H], BF16)
        self.membf = sb("membf", [128, 128], BF16)
        self.gate_sb = sb("gate_sb", [128, NK * cfg.E], BF16)
        self.ident_sb = sb("ident_sb", [128, 128], BF16)
        self.mask_sb = sb("mask_sb", [128, 512 + (cfg.NJ - 1) * 128], BF16)
        self.ln_attn = sb("ln_attn", [128, NTC * D], BF16)
        ra = sb("ra", [128, NK * T], F32)
        self.acc = ra.ap()
        half = (NK * T) // 2
        self.x_ln_fm = ra.ap()[:, 0:half].bitcast(BF16)
        self.att_all = ra.ap()[:, half:2 * half].bitcast(BF16)
        self.ao_tm = self.x_ln_fm
        self.stage = [ra.ap()[:, half + i * D: half + (i + 1) * D]
                      for i in range(2)]
        self.x1_ln_fm = sb("x1_ln_fm", [128, NK * T], BF16)
        self.cb_ring = sb("cb_ring", [128, N_CB * NK * 128], BF16)
        self.rbw = max(cfg.FEC, cfg.FSHC // 2) * 512
        self.rb_ring = sb("rb_ring", [128, N_RB * self.rbw], BF16)
        self.q_db = self.rb_ring.ap()[:, 0:T]
        self.k_db = self.rb_ring.ap()[:, T:2 * T]
        actw = max(cfg.FEC, cfg.FSHC // 2) * T
        self.act_e = sb("act_e", [128, actw], BF16)
        self.act_sh = self.act_e
        self.silu_s = sb("silu_s", [128, T], BF16)
        self.combine = sb("combine", [128, NTC * cfg.E], F32)
        self.q_fm = sb("q_fm", [128, T], BF16)
        self.k_fm = sb("k_fm", [128, T], BF16)

        self.sq_fm = sb("sq_fm", [128, T], BF16)
        self.sk_fm = sb("sk_fm", [128, T], BF16)
        self.v_tm = sb("v_tm", [128, T], BF16)
        self.sk_tm = sb("sk_tm", [128, T], BF16)
        scr4 = sb("scr4", [128, 2 * T], BF16)
        self.e_t1 = scr4.ap()[:, 0:T]
        self.e_t2 = scr4.ap()[:, T:2 * T]
        self.pt_sb = scr4.ap()[:, 0:cfg.NJ * 512]
        self.ao_fm_s = scr4.ap()[:, 0:T]
        self.p_sb = sb("p_sb", [128, cfg.NJ * 512], BF16)
        self.v_fm = self.p_sb.ap()[:, 0:T]
        self.adot_tm = sb("adot_tm", [128, 512], BF16)
        self.adot_fm = sb("adot_fm", [128, 512], BF16)
        self.dmd_tm = sb("dmd_tm", [128, 512], BF16)
        self.negdel = sb("negdel", [128, 512], BF16)
        self.recips = sb("recips", [128, 16], F32)
        self.sumrow = sb("sumrow", [128, cfg.NJ], F32)
        self.negm = sb("negm", [128, cfg.NJ], F32)

        self.zsum = sb("zsum", [128, 1], F32)
        self.bn24 = sb("bn24", [128, 24], F32)
        self.mv2 = sb("mv2", [128, 2], F32)
        self.std1 = sb("std1", [128, 1], F32)
        self.negmr = sb("negmr", [128, 1], F32)
        self.eps_sb = sb("eps_sb", [128, 1], F32)
        self.tick_v = sb("tick_v", [1, 4], F32)
        self.tick_s = sb("tick_s", [1, 4], F32)
        self.rstd1 = sb("rstd1", [128, 1], F32)
        self.xln_tm = sb("xln_tm", [128, D], BF16)
        self.top8 = sb("top8", [128, 8], F32)
        self.gsc = sb("gsc", [128, max(cfg.E, cfg.H)], F32)
        self.gsel = sb("gsel", [128, cfg.E], F32)
        self.gsum = sb("gsum", [128, 1], F32)
        self.grcp = sb("grcp", [128, 1], F32)

        self.psum = nc.alloc_psum_tensor("psum", [128, 4096], F32)

        self.s_pe = self.sem("s_pe")
        self.s_dve = self.sem("s_dve")
        self.s_act = self.sem("s_act")
        self.s_cb = [self.sem(f"s_cb{i}") for i in range(N_CB)]
        self.s_rb = [self.sem(f"s_rb{i}") for i in range(N_RB)]
        self.s_wq = self.sem("s_wq")
        self.s_xq = self.sem("s_xq")
        self.s_st = self.sem("s_st")

        self.cb_free = [(None, 0)] * N_CB
        self.rb_free = [(None, 0)] * N_RB
        self.cb_next = 0

        self.acc_store_tok = [0] * cfg.NTC
        self.lnattn_tok = [0] * cfg.NTC
        self.attall_lastuse_pe = 0
        self.xlnfm_lastuse_pe = 0
        self.x1lnfm_lastuse_pe = 0
        self.xln_slot_free = [(None, 0), (None, 0)]
        self.stage_slot_free = [(None, 0), (None, 0)]
        self.memupd_tok = [0] * cfg.H
        self.membf_lastuse_pe = 0
        self.last_membf_act = 0
        self.psb_lastuse_pe = 0
        self.combine_tok = [0] * cfg.NTC
        self.acc_ready_tok = [(None, 0)] * cfg.NTC
        self.prev_head_dve = 0
        self.att_tok_last = 0
        self.att_tail = None
        self.tail_trans_tok = 0
        self.fill_queue = []
        self.qk_evac = {}
        self.qk_lastuse_pe = 0

    def ps(self, c0, c1):
        return self.psum.ap()[:, c0:c1]

    def ps_bf(self, c0, nbf):
        return self.psum.ap()[:, c0:c0 + (nbf + 1) // 2].bitcast(BF16)

    def cb(self, i):
        w = self.cfg.NK * 128
        return self.cb_ring.ap()[:, i * w:(i + 1) * w]

    def rb(self, i):
        return self.rb_ring.ap()[:, i * self.rbw:(i + 1) * self.rbw]

    def dbg_tap(self, name, src_ap, tok):
        if not self.cfg.dbg or name in self.dbg_outs:
            return
        self.dbg_outs.add(name)
        nc, SYN = self.nc, self.nc.sync
        d = nc.dram_tensor("dbg_" + name, list(src_ap.shape), src_ap.dtype,
                           kind="ExternalOutput").ap()
        sem, t = tok
        self.wait(SYN, sem, t)
        self.inc(SYN.dma_start(out=d, in_=src_ap), self.s_st, 16)

    def load_cb(self, src_ap):
        POO = self.nc.gpsimd
        i = self.cb_next
        self.cb_next = (i + 1) % N_CB
        sem, tok = self.cb_free[i]
        self.wait(POO, sem, tok)
        rdy = self.inc(POO.dma_start(out=self.cb(i), in_=src_ap),
                       self.s_cb[i], 16)
        return i, (self.s_cb[i], rdy)

    def load_rb(self, i, src_ap, width):
        POO = self.nc.gpsimd
        sem, tok = self.rb_free[i]
        self.wait(POO, sem, tok)
        rdy = self.inc(POO.dma_start(out=self.rb(i)[:, 0:width], in_=src_ap),
                       self.s_rb[i], 16)
        return (self.s_rb[i], rdy)

    def transpose_batch(self, srcs, bank_c0, dst_ap=None, tok_req=None,
                        act_pre=None):
        TEN, SCA = self.nc.tensor, self.nc.scalar
        n = len(srcs)
        assert n <= 8
        bks = banks_of(bank_c0, bank_c0 + 64 * n)
        self.pe_claim(bks)
        self.wait(TEN, self.s_xq, self.t_ident)
        if tok_req is not None:
            self.wait(TEN, *tok_req)
        ins = None
        for i, s in enumerate(srcs):
            ins = TEN.transpose(self.ps_bf(bank_c0 + 64 * i, 128), s,
                                self.ident_sb[:])
        ptok = self.inc(ins, self.s_pe)
        atok = None
        if dst_ap is not None:
            if act_pre is not None:
                self.wait(SCA, *act_pre)
            self.wait(SCA, self.s_pe, ptok)
            atok = self.inc(SCA.copy(dst_ap, self.ps_bf(bank_c0, 128 * n)),
                            self.s_act)
            self.bank_read(bks, self.s_act, atok)
        return ptok, atok

    def vflush(self):
        VEC = self.nc.vector
        VEC.drain()
        return self.inc(VEC.memset(self.tick_v[:], 0.0), self.s_dve)

    def sflush(self):
        SCA = self.nc.scalar
        SCA.drain()
        return self.inc(SCA.memzero(self.tick_s[:]), self.s_act)

    def ln_stats(self, x_ap, nch):
        VEC, SCA = self.nc.vector, self.nc.scalar
        for c in range(nch):
            VEC.bn_stats(self.bn24[:, 6 * c:6 * (c + 1)],
                         x_ap[:, 512 * c:512 * (c + 1)])
        VEC.drain()
        t = self.inc(VEC.bn_aggr(self.mv2[:], self.bn24[:, 0:6 * nch]),
                     self.s_dve)
        self.wait(SCA, self.s_dve, t)
        t = self.inc(SCA.activation(self.std1[:], self.mv2[:, 1:2], AF.Sqrt,
                                    bias=self.eps_sb[:]), self.s_act)
        self.wait(VEC, self.s_act, t)
        VEC.reciprocal(self.rstd1[:], self.std1[:])
        VEC.drain()
        VEC.tensor_scalar(self.negmr[:], self.mv2[:, 0:1], self.rstd1[:],
                          -1.0, ALU.mult, ALU.mult)
        t = self.vflush()
        return t

    def ln_apply(self, out_ap, in_ap, t_stats):
        SCA = self.nc.scalar
        self.wait(SCA, self.s_dve, t_stats)
        return self.inc(SCA.activation(out_ap, in_ap, AF.Identity,
                                       bias=self.negmr[:],
                                       scale=self.rstd1[:]), self.s_act)

    def emit(self):
        cfg, nc = self.cfg, self.nc
        VEC, SCA, SYN, POO = nc.vector, nc.scalar, nc.sync, nc.gpsimd
        self.t_ident = self.inc(
            SYN.dma_start(out=self.ident_sb[:], in_=self.ident_d.ap()),
            self.s_xq, 16)
        self.t_mask = self.inc(
            SYN.dma_start(out=self.mask_sb[:], in_=self.mask_d.ap()),
            self.s_xq, 16)
        t_beta = self.inc(SYN.dma_start(out=self.gsc[:, 0:cfg.H],
                                        in_=self.betas_r), self.s_xq, 16)
        self.t_gate = self.inc(POO.dma_start(out=self.gate_sb[:],
                                             in_=self.gate_r), self.s_wq, 16)
        VEC.memset(self.eps_sb[:], LN_EPS)
        self.wait(SCA, self.s_xq, t_beta)
        self.t_beta_sig = self.inc(
            SCA.activation(self.beta_sb[:], self.gsc[:, 0:cfg.H],
                           AF.Sigmoid), self.s_act)

        for q in range(cfg.B):
            self.wait(VEC, self.s_act, self.last_membf_act)
            self.wait(VEC, self.s_pe, self.membf_lastuse_pe)
            VEC.memset(self.mem_sb[:], 0.0)
            VEC.memset(self.z_sb[:], 1.0 / 128.0)
            t0 = self.inc(VEC.memset(self.z_bf[:], 1.0 / 128.0), self.s_dve)
            for h in range(cfg.H):
                self.memupd_tok[h] = t0
            for g in range(cfg.NGRP):
                self.emit_group(q, g)
        SYN.wait_ge(self.s_st.h, self.s_st.c)

    def emit_group(self, q, g):
        cfg, nc = self.cfg, self.nc
        TEN, VEC, SCA = nc.tensor, nc.vector, nc.scalar
        SYN, POO = nc.sync, nc.gpsimd
        D, NK, H, T, NTC, NJ, E = (cfg.D, cfg.NK, cfg.H, cfg.T, cfg.NTC,
                                   cfg.NJ, cfg.E)
        FEC, FSHC, NDQ = cfg.FEC, cfg.FSHC, cfg.NDQ
        s_pe, s_dve, s_act = self.s_pe, self.s_dve, self.s_act
        s_wq, s_xq, s_st = self.s_wq, self.s_xq, self.s_st
        ps = self.ps
        tok0 = g * T
        nhalf = (NK + 7) // 8

        st_lo = max(self.acc_store_tok[:NTC // 2])
        for tc in range(NTC):
            sl = tc % 2
            xsl = 0
            stg = self.stage[sl]
            self.wait(SYN, s_st, self.acc_store_tok[NTC // 2 + sl])
            self.wait(SYN, s_pe, self.attall_lastuse_pe)
            sem, tkf = self.stage_slot_free[sl]
            self.wait(SYN, sem, tkf)
            xin = self.x_in[q, tok0 + tc * 128: tok0 + (tc + 1) * 128, :]
            t_x = self.inc(SYN.dma_start(out=stg, in_=xin), s_xq, 16)
            self.wait(VEC, s_xq, t_x)
            t_stats = self.ln_stats(stg, D // 512)
            semx, tkx = self.xln_slot_free[xsl]
            self.wait(SCA, semx, tkx)
            xlns = self.xln_tm.ap()[:, 0:D]
            ta = self.ln_apply(xlns, stg, t_stats)
            self.stage_slot_free[sl] = (s_act, ta)
            for half in range(nhalf):
                kk = min(8, NK - half * 8)
                srcs = [self.xln_tm.ap()[:, k * 128:(k + 1) * 128]
                        for k in range(half * 8, half * 8 + kk)]
                dst = self.x_ln_fm.rearrange("p (k t) -> p k t", k=NK) \
                    [:, half * 8:half * 8 + kk, tc * 128:(tc + 1) * 128]
                self.wait(TEN, s_pe, self.xlnfm_lastuse_pe)
                _, atok = self.transpose_batch(
                    srcs, 3072 if half % 2 == 0 else 3584, dst_ap=dst,
                    tok_req=(s_act, ta), act_pre=(s_st, st_lo))
                self.xln_slot_free[xsl] = (s_act, atok)
        t_xlnfm_ready = max(t for _, t in self.xln_slot_free)
        self.dbg_tap("x_ln_fm", self.x_ln_fm[:, :], (s_act, t_xlnfm_ready))
        if cfg.stop == "a1":
            return

        st_hi = max(self.acc_store_tok[NTC // 2:])
        self.att_tail = None
        self.fill_queue = []
        nH = cfg.nheads or H

        def qk_piece(hh, name, hf, c0b=None):
            wsrc = self.wq_r if name == "q" else self.wk_r
            buf = ((self.q_fm[:, :] if name == "q" else self.k_fm[:, :])
                   if hh % 2 == 0 else
                   (self.q_db if name == "q" else self.k_db))
            key = (hh, name)
            if key not in self._qk_slot:
                self._qk_slot[key] = self.load_cb(wsrc[hh])
            sl, trdy = self._qk_slot[key]
            c0 = (0 if name == "q" else 1024) if c0b is None else c0b - hf * 512
            bank = [c0 // 512 + hf]
            self.pe_claim(bank)
            self.wait(TEN, *trdy)
            self.wait(TEN, s_act, t_xlnfm_ready)
            ins = None
            for k in range(NK):
                ins = TEN.matmul(
                    ps(c0 + hf * 512, c0 + (hf + 1) * 512),
                    self.cb(sl)[:, k * 128:(k + 1) * 128],
                    self.x_ln_fm[:, k * T + hf * 512: k * T + (hf + 1) * 512],
                    start=(k == 0), stop=(k == NK - 1))
            ptok = self.inc(ins, s_pe)
            self.cb_free[sl] = (s_pe, ptok)
            self.xlnfm_lastuse_pe = ptok
            self.wait(SCA, s_dve, self.prev_head_dve)
            self.wait(SCA, s_pe, ptok)
            etok = self.inc(SCA.copy(buf[:, hf * 512:(hf + 1) * 512],
                                     ps(c0 + hf * 512, c0 + (hf + 1) * 512)),
                            s_act)
            self.bank_read(bank, s_act, etok)
            self.qk_evac[key] = max(self.qk_evac.get(key, 0), etok)

        self._qk_slot = {}
        for h in range(nH):
            qb = self.q_fm[:, :] if h % 2 == 0 else self.q_db
            kb = self.k_fm[:, :] if h % 2 == 0 else self.k_db
            if h == 0:
                for name in ("q", "k"):
                    for hf in range(2):
                        qk_piece(0, name, hf)
            sl_v, tv = self.load_cb(self.wv_r[h])
            self.wait(SCA, s_dve, self.prev_head_dve)
            etok_v = 0
            for hf in range(2):
                bank = [hf]
                self.pe_claim(bank)
                self.wait(TEN, *tv)
                self.wait(TEN, s_act, t_xlnfm_ready)
                ins = None
                for k in range(NK):
                    ins = TEN.matmul(
                        ps(hf * 512, (hf + 1) * 512),
                        self.cb(sl_v)[:, k * 128:(k + 1) * 128],
                        self.x_ln_fm[:, k * T + hf * 512:
                                     k * T + (hf + 1) * 512],
                        start=(k == 0), stop=(k == NK - 1))
                ptok = self.inc(ins, s_pe)
                self.cb_free[sl_v] = (s_pe, ptok)
                self.xlnfm_lastuse_pe = ptok
                self.wait(SCA, s_pe, ptok)
                etok_v = self.inc(SCA.copy(self.v_fm[:, hf * 512:(hf + 1) * 512],
                                           ps(hf * 512, (hf + 1) * 512)),
                                  s_act)
                self.bank_read(bank, s_act, etok_v)
            sig_tok = {}
            for name, srcb, dst in (("q", qb, self.sq_fm),
                                    ("k", kb, self.sk_fm)):
                self.wait(VEC, s_act, self.qk_evac[(h, name)])
                t1 = self.inc(VEC.tensor_scalar_min(self.e_t1, srcb,
                                                    0.0), s_dve)
                self.wait(SCA, s_dve, t1)
                t2 = self.inc(SCA.activation(self.e_t2, self.e_t1,
                                             AF.Exp), s_act)
                self.wait(VEC, s_act, t2)
                sig_tok[name] = self.inc(VEC.scalar_tensor_tensor(
                    dst[:], srcb, 0.0, self.e_t2, ALU.max, ALU.add),
                    s_dve)
            _, t_vtm = self.transpose_batch(
                [self.v_fm[:, c * 128:(c + 1) * 128] for c in range(NTC)],
                1024, dst_ap=self.v_tm[:, 0:NTC * 128],
                tok_req=(s_act, etok_v))
            _, t_sktm = self.transpose_batch(
                [self.sk_fm[:, c * 128:(c + 1) * 128] for c in range(NTC)],
                1536, dst_ap=self.sk_tm[:, 0:NTC * 128],
                tok_req=(s_dve, sig_tok["k"]))
            if h == 0:
                self.dbg_tap("q_fm", self.q_fm[:, :],
                             (s_act, self.qk_evac[(0, "q")]))
                self.dbg_tap("sq_fm", self.sq_fm[:, :], (s_dve, sig_tok["q"]))
                self.dbg_tap("v_tm", self.v_tm[:, :], (s_act, t_vtm))
            if h + 1 < nH:
                for name in ("q", "k"):
                    for hf in range(2):
                        self.fill_queue.append(
                            (lambda hh=h + 1, n=name, f=hf, c0b=None:
                             qk_piece(hh, n, f, c0b)))
            for seg in range(2):
                self.att_tail = self.emit_attn_core(
                    h, seg, qb, kb, sig_tok, t_vtm, t_sktm, st_hi,
                    filler=self.att_tail)
            self.prev_head_dve = self.memupd_tok[h]
            for p in self.fill_queue:
                p()
            self.fill_queue = []
        if self.att_tail is not None:
            self.att_tail()
            self.att_tail = None
        self.qk_lastuse_pe = self.s_pe.c
        self.rb_free = [(self.s_pe, max(t for _, t in
                                        [(None, self.qk_lastuse_pe)] +
                                        [f for f in self.rb_free]))] * N_RB
        self.dbg_tap("att_all", self.att_all[:, :],
                     (self.s_dve, self.att_tok_last))
        if cfg.stop == "heads":
            return

        t_ao = 0
        for m in range(NK):
            sl_o, to = self.load_cb(self.wo_r[m])
            c0 = 0 if (m % 2 == 0) else 1024
            bks = banks_of(c0, c0 + 1024)
            self.pe_claim(bks)
            self.wait(TEN, *to)
            self.wait(TEN, s_dve, self.att_tok_last)
            ins = None
            for k in range(NK):
                for hf in range(2):
                    ins = TEN.matmul(
                        ps(c0 + hf * 512, c0 + (hf + 1) * 512),
                        self.cb(sl_o)[:, k * 128:(k + 1) * 128],
                        self.att_all[:, k * T + hf * 512:
                                     k * T + (hf + 1) * 512],
                        start=(k == 0), stop=(k == NK - 1))
            ptok = self.inc(ins, s_pe)
            self.cb_free[sl_o] = (s_pe, ptok)
            self.attall_lastuse_pe = ptok
            self.wait(SCA, s_pe, ptok)
            etok = self.inc(SCA.copy(self.ao_fm_s, ps(c0, c0 + 1024)),
                            s_act)
            self.bank_read(bks, s_act, etok)
            srcs = [self.ao_fm_s[:, tc * 128:(tc + 1) * 128]
                    for tc in range(NTC)]
            dst = self.ao_tm.rearrange("p (t d) -> p t d", t=NTC) \
                [:, :, m * 128:(m + 1) * 128]
            _, t_ao = self.transpose_batch(srcs, 3072, dst_ap=dst,
                                           tok_req=(s_act, etok))
        self.dbg_tap("ao_tm", self.ao_tm[:, 0:NTC * D], (s_act, t_ao))

        for tc in range(NTC):
            src = self.ao_tm[:, tc * D:(tc + 1) * D]
            self.wait(VEC, s_act, t_ao)
            t_stats = self.ln_stats(src, D // 512)
            self.lnattn_tok[tc] = self.ln_apply(
                self.ln_attn[:, tc * D:(tc + 1) * D], src, t_stats)
        self.dbg_tap("ln_attn", self.ln_attn[:, :],
                     (s_act, self.lnattn_tok[NTC - 1]))
        if cfg.stop == "attn":
            return

        for tc in range(NTC):
            acc_t = self.acc[:, tc * D:(tc + 1) * D]
            if tc < NTC // 2:
                self.wait(SYN, s_act, self.lnattn_tok[2 * tc + 1])
            else:
                self.wait(SYN, s_pe, self.attall_lastuse_pe)
                if tc - NTC // 2 < 2:
                    sem, tkf = self.stage_slot_free[tc - NTC // 2]
                    self.wait(SYN, sem, tkf)
            self.wait(SYN, s_st, self.acc_store_tok[tc])
            xin = self.x_in[q, tok0 + tc * 128: tok0 + (tc + 1) * 128, :]
            t_x = self.inc(SYN.dma_start(out=acc_t, in_=xin), s_xq, 16)
            self.wait(VEC, s_xq, t_x)
            self.wait(VEC, s_act, self.lnattn_tok[tc])
            VEC.tensor_add(acc_t, acc_t, self.ln_attn[:, tc * D:(tc + 1) * D])
            t_stats = self.ln_stats(acc_t, D // 512)
            sl = 0
            semx, tkx = self.xln_slot_free[sl]
            self.wait(SCA, semx, tkx)
            xlns = self.xln_tm.ap()[:, 0:D]
            ta = self.ln_apply(xlns, acc_t, t_stats)
            for half in range(nhalf):
                kk = min(8, NK - half * 8)
                srcs = [self.xln_tm.ap()[:, k * 128:(k + 1) * 128]
                        for k in range(half * 8, half * 8 + kk)]
                dst = self.x1_ln_fm.rearrange("p (k t) -> p k t", k=NK) \
                    [:, half * 8:half * 8 + kk, tc * 128:(tc + 1) * 128]
                self.wait(TEN, s_pe, self.x1lnfm_lastuse_pe)
                _, atok = self.transpose_batch(
                    srcs, 3072 if half % 2 == 0 else 3584, dst_ap=dst,
                    tok_req=(s_act, ta))
                self.xln_slot_free[sl] = (s_act, atok)
            t_x1tc = self.xln_slot_free[sl][1]
            self.pe_claim([3])
            self.wait(TEN, s_act, t_x1tc)
            self.wait(TEN, s_wq, self.t_gate)
            ins = None
            for k in range(NK):
                ins = TEN.matmul(ps(1536, 1536 + E),
                                 self.x1_ln_fm[:, k * T + tc * 128:
                                               k * T + (tc + 1) * 128],
                                 self.gate_sb[:, k * E:(k + 1) * E],
                                 start=(k == 0), stop=(k == NK - 1))
            tg = self.inc(ins, s_pe)
            self.wait(SCA, s_pe, tg)
            SCA.activation(self.gsc[:, 0:E], ps(1536, 1536 + E), AF.Sigmoid)
            tsg = self.sflush()
            self.bank_read([3], s_act, tsg)
            self.wait(VEC, s_act, tsg)
            VEC.max(out=self.top8[:], in_=self.gsc[:, 0:E])
            VEC.drain()
            VEC.tensor_scalar(self.gsel[:], self.gsc[:, 0:E],
                              self.top8[:, cfg.TOPK - 1:cfg.TOPK], None,
                              ALU.is_ge)
            VEC.drain()
            VEC.tensor_mul(self.gsel[:], self.gsel[:], self.gsc[:, 0:E])
            VEC.drain()
            VEC.reduce_sum(self.gsum[:], self.gsel[:], axis=AX)
            VEC.drain()
            VEC.tensor_scalar(self.gsum[:], self.gsum[:], 1.0e-20, None,
                              ALU.add)
            VEC.drain()
            VEC.reciprocal(self.grcp[:], self.gsum[:])
            VEC.drain()
            self.combine_tok[tc] = self.inc(VEC.tensor_scalar(
                self.combine[:, tc * E:(tc + 1) * E], self.gsel[:],
                self.grcp[:], None, ALU.mult), s_dve)
        t_x1_ready = max(t for _, t in self.xln_slot_free)
        self.dbg_tap("x1_ln_fm", self.x1_ln_fm[:, :], (s_act, t_x1_ready))
        self.dbg_tap("combine", self.combine[:, :],
                     (s_dve, self.combine_tok[NTC - 1]))
        if cfg.stop == "b1":
            return

        for half in range(2):
            nfc = FSHC // 2
            gu_tok = [0] * nfc
            for fl in range(nfc):
                fc = half * nfc + fl
                sl_g, tg_ = self.load_cb(self.sg_r[fc])
                sl_u, tu_ = self.load_cb(self.su_r[fc])
                gu_tok[fl] = self.emit_gu_act(sl_g, tg_, sl_u, tu_,
                                              t_x1_ready, self.act_sh, fl)
            self.emit_down(
                nfc, self.act_sh, gu_tok,
                lambda dq, hh=half, n=nfc: self.sd_r[dq][:, hh * n * 512:
                                                         (hh + 1) * n * 512],
                nfc * 512, first_write=(half == 0), comb_e=None)
        if cfg.stop == "shared":
            sem, t_acc = self.acc_ready_tok[0]
            self.dbg_tap("accsh", self.acc[:, 0:D], (sem, t_acc))
            return
        nE = cfg.nexp or E
        for e in range(nE):
            gu_tok = [0] * FEC
            for fc in range(FEC):
                sl_g, tg_ = self.load_cb(self.wg_r[e, fc])
                sl_u, tu_ = self.load_cb(self.wu_r[e, fc])
                gu_tok[fc] = self.emit_gu_act(sl_g, tg_, sl_u, tu_,
                                              t_x1_ready, self.act_e, fc)
            if e == 0:
                self.dbg_tap("act_e0", self.act_e[:, :],
                             (self.s_dve, gu_tok[FEC - 1]))
            self.emit_down(FEC, self.act_e, gu_tok,
                           lambda dq, e=e: self.wd_r[e, dq],
                           FEC * 512, first_write=False, comb_e=e)
        self.dbg_tap("acc0", self.acc[:, 0:D], self.acc_ready_tok[0])
        if cfg.stop == "moe":
            return

        for tc in range(NTC):
            acc_t = self.acc[:, tc * D:(tc + 1) * D]
            sem, t_acc = self.acc_ready_tok[tc]
            self.wait(VEC, sem, t_acc)
            t_stats = self.ln_stats(acc_t, D // 512)
            sl = 0
            semx, tkx = self.xln_slot_free[sl]
            self.wait(SCA, semx, tkx)
            xlns = self.xln_tm.ap()[:, 0:D]
            ta = self.ln_apply(xlns, acc_t, t_stats)
            self.wait(SYN, s_act, ta)
            xin = self.x_in[q, tok0 + tc * 128: tok0 + (tc + 1) * 128, :]
            t_x = self.inc(SYN.dma_start(out=acc_t, in_=xin), s_xq, 16)
            self.wait(VEC, s_xq, t_x)
            self.wait(VEC, s_act, ta)
            VEC.tensor_add(acc_t, acc_t, self.ln_attn[:, tc * D:(tc + 1) * D])
            tout = self.inc(VEC.tensor_add(acc_t, acc_t, xlns), s_dve)
            self.xln_slot_free[sl] = (s_dve, tout)
            self.wait(SYN, s_dve, tout)
            yout = self.y_out[q, tok0 + tc * 128: tok0 + (tc + 1) * 128, :]
            self.acc_store_tok[tc] = self.inc(
                SYN.dma_start(out=yout, in_=acc_t), s_st, 16)
            self.acc_ready_tok[tc] = (None, 0)

    def emit_gu_act(self, sl_g, tg, sl_u, tu, t_x1, act_buf, fslot):
        cfg, nc = self.cfg, self.nc
        TEN, VEC, SCA = nc.tensor, nc.vector, nc.scalar
        NK, T = cfg.NK, cfg.T
        ps = self.ps
        s_pe, s_dve, s_act = self.s_pe, self.s_dve, self.s_act
        toks = {}
        for c0, sl, trdy in ((0, sl_g, tg), (1024, sl_u, tu)):
            bks = banks_of(c0, c0 + 1024)
            self.pe_claim(bks)
            self.wait(TEN, *trdy)
            self.wait(TEN, s_act, t_x1)
            ins = None
            for k in range(NK):
                for hf in range(2):
                    ins = TEN.matmul(
                        ps(c0 + hf * 512, c0 + (hf + 1) * 512),
                        self.cb(sl)[:, k * 128:(k + 1) * 128],
                        self.x1_ln_fm[:, k * T + hf * 512:
                                      k * T + (hf + 1) * 512],
                        start=(k == 0), stop=(k == NK - 1))
            ptok = self.inc(ins, s_pe)
            self.cb_free[sl] = (s_pe, ptok)
            self.x1lnfm_lastuse_pe = ptok
            toks[c0] = ptok
        self.wait(SCA, s_pe, toks[0])
        t_silu = self.inc(SCA.activation(self.silu_s[:], ps(0, 1024),
                                         AF.Silu), s_act)
        self.bank_read([0, 1], s_act, t_silu)
        self.wait(VEC, s_act, t_silu)
        self.wait(VEC, s_pe, toks[1024])
        t_act = self.inc(VEC.tensor_mul(
            act_buf[:, fslot * T:(fslot + 1) * T], self.silu_s[:],
            ps(1024, 2048)), s_dve)
        self.bank_read([2, 3], s_dve, t_act)
        return t_act

    def emit_down(self, nfc, act_buf, gu_tok, src_fn, width, first_write,
                  comb_e):
        cfg, nc = self.cfg, self.nc
        TEN, VEC = nc.tensor, nc.vector
        NTC, T, D, NDQ, E = cfg.NTC, cfg.T, cfg.D, cfg.NDQ, cfg.E
        ps = self.ps
        s_pe, s_dve = self.s_pe, self.s_dve
        for dq in range(NDQ):
            slot = dq % N_RB
            trb = self.load_rb(slot, src_fn(dq), width)
            for tc in range(NTC):
                self.pe_claim([tc])
                self.wait(TEN, *trb)
                ins = None
                for fl in range(nfc):
                    self.wait(TEN, s_dve, gu_tok[fl])
                    ins = TEN.matmul(
                        ps(tc * 512, (tc + 1) * 512),
                        act_buf[:, fl * T + tc * 128:
                                fl * T + (tc + 1) * 128],
                        self.rb(slot)[:, fl * 512:(fl + 1) * 512],
                        start=(fl == 0), stop=(fl == nfc - 1))
                ptok = self.inc(ins, s_pe)
                self.rb_free[slot] = (s_pe, ptok)
                self.wait(VEC, s_pe, ptok)
                dst = self.acc[:, tc * D + dq * 512: tc * D + (dq + 1) * 512]
                if first_write:
                    ins = VEC.tensor_copy(dst, ps(tc * 512, (tc + 1) * 512))
                elif comb_e is None:
                    ins = VEC.tensor_add(dst, dst,
                                         ps(tc * 512, (tc + 1) * 512))
                else:
                    self.wait(VEC, s_dve, self.combine_tok[tc])
                    ins = VEC.scalar_tensor_tensor(
                        dst, ps(tc * 512, (tc + 1) * 512),
                        self.combine[:, tc * E + comb_e: tc * E + comb_e + 1],
                        dst, ALU.mult, ALU.add)
                etok = self.inc(ins, s_dve)
                self.bank_read([tc], s_dve, etok)
                self.acc_ready_tok[tc] = (s_dve, etok)

    def emit_attn_core(self, h, seg, qb, kb, sig_tok, t_vtm, t_sktm, st_hi,
                       filler=None):
        cfg, nc = self.cfg, self.nc
        TEN, VEC, SCA = nc.tensor, nc.vector, nc.scalar
        NJ, T = cfg.NJ, cfg.T
        ps = self.ps
        s_pe, s_dve, s_act, s_st = self.s_pe, self.s_dve, self.s_act, self.s_st
        sbase = seg * 512

        self.wait(SCA, s_dve, self.memupd_tok[h])
        self.wait(SCA, s_pe, self.membf_lastuse_pe)
        SCA.copy(self.membf[:], self.mem_sb[:, h * 128:(h + 1) * 128])
        t_mbf = self.sflush()
        self.last_membf_act = t_mbf

        self.pe_claim([0, 1, 2, 3])
        ins = None
        for j in range(NJ):
            ins = TEN.matmul(ps(j * 512, (j + 1) * 512),
                             qb[:, sbase + j * 128: sbase + (j + 1) * 128],
                             kb[:, sbase:sbase + 512],
                             start=True, stop=True)
        t_sc = self.inc(ins, s_pe)

        self.pe_claim([5, 6, 7])
        self.wait(TEN, s_act, t_mbf)
        self.wait(TEN, s_dve, sig_tok["q"])
        self.wait(TEN, s_dve, sig_tok["k"])
        for j in range(NJ):
            TEN.matmul(ps(2560 + j * 128, 2560 + (j + 1) * 128),
                       self.sq_fm[:, sbase + j * 128: sbase + (j + 1) * 128],
                       self.membf[:], start=True, stop=True)
            TEN.matmul(ps(3072 + j * 128, 3072 + (j + 1) * 128),
                       self.sk_fm[:, sbase + j * 128: sbase + (j + 1) * 128],
                       self.membf[:], start=True, stop=True)
            TEN.matmul(ps(3584 + j, 3585 + j),
                       self.sq_fm[:, sbase + j * 128: sbase + (j + 1) * 128],
                       self.z_bf[:, h:h + 1], start=True, stop=True)
            ins = TEN.matmul(ps(3588 + j, 3589 + j),
                             self.sk_fm[:, sbase + j * 128:
                                        sbase + (j + 1) * 128],
                             self.z_bf[:, h:h + 1], start=True, stop=True)
        t_sig = self.inc(ins, s_pe)
        self.membf_lastuse_pe = t_sig

        if filler is not None:
            filler()
        if self.fill_queue:
            self.fill_queue.pop(0)(c0b=2048)

        self.wait(VEC, s_pe, t_sc)
        self.wait(VEC, self.s_xq, self.t_mask)
        for j in range(NJ):
            VEC.reduce_max(self.negm[:, j:j + 1],
                           ps(j * 512, j * 512 + 128 * (j + 1)),
                           axis=AX, negate=True)
        VEC.drain()
        VEC.tensor_scalar_mul(self.negm[:, 0:NJ], self.negm[:, 0:NJ],
                              INV_SQRT_DK)
        smj = self.vflush()

        exp_tok = [0] * NJ
        for j in range(NJ):
            self.wait(SCA, s_dve, smj)
            self.wait(SCA, s_pe, self.psb_lastuse_pe)
            exp_tok[j] = self.inc(SCA.activation(
                self.p_sb[:, j * 512:(j + 1) * 512], ps(j * 512, (j + 1) * 512),
                AF.Exp, scale=INV_SQRT_DK, bias=self.negm[:, j:j + 1]), s_act)
        self.bank_read([0, 1, 2, 3], s_act, exp_tok[NJ - 1])

        for j in range(NJ):
            self.wait(VEC, s_act, exp_tok[j])
            moff = (NJ - 1 - j) * 128
            VEC.scalar_tensor_tensor(
                self.p_sb[:, j * 512:(j + 1) * 512],
                self.p_sb[:, j * 512:(j + 1) * 512], 1.0,
                self.mask_sb[:, moff:moff + 512], ALU.mult, ALU.mult,
                accum_out=self.sumrow[:, j:j + 1])
        t_pm = self.vflush()

        self.wait(VEC, s_pe, t_sig)
        VEC.reciprocal(self.recips[:, 0:NJ], ps(3584, 3584 + NJ))
        VEC.reciprocal(self.recips[:, 4:4 + NJ], ps(3588, 3588 + NJ))
        VEC.reciprocal(self.recips[:, 8:8 + NJ], self.sumrow[:, 0:NJ])
        VEC.drain()
        self.wait(VEC, s_act, t_vtm)
        for j in range(NJ):
            VEC.scalar_tensor_tensor(
                self.negdel[:, j * 128:(j + 1) * 128],
                ps(3072 + j * 128, 3072 + (j + 1) * 128),
                self.recips[:, 4 + j:5 + j],
                self.v_tm[:, (seg * NJ + j) * 128: (seg * NJ + j + 1) * 128],
                ALU.mult, ALU.subtract)
        t_nd = self.vflush()
        self.bank_read([6], s_dve, t_nd)
        self.bank_read([7], s_dve, t_nd)

        self.pe_claim([7])
        self.wait(TEN, s_dve, t_nd)
        self.wait(TEN, s_act, t_sktm)
        ins = None
        for j in range(NJ):
            ins = TEN.matmul(ps(3712, 3840),
                             self.sk_tm[:, (seg * NJ + j) * 128:
                                        (seg * NJ + j + 1) * 128],
                             self.negdel[:, j * 128:(j + 1) * 128],
                             start=(j == 0), stop=(j == NJ - 1))
        t_upd = self.inc(ins, s_pe)

        pt_atok = 0
        for cpair in range(2):
            srcs = [self.p_sb[:, j * 512 + c * 128: j * 512 + (c + 1) * 128]
                    for c in (2 * cpair, 2 * cpair + 1) for j in range(NJ)]
            dst = self.pt_sb[:, cpair * 1024:(cpair + 1) * 1024]
            pt_ptok, pt_atok = self.transpose_batch(
                srcs, 0 if cpair == 0 else 512, dst_ap=dst,
                tok_req=(s_dve, t_pm))
        self.psb_lastuse_pe = pt_ptok
        self.pe_claim([4])
        self.wait(TEN, s_act, pt_atok)
        ins = None
        for j in range(NJ):
            for c in range(NJ):
                ins = TEN.matmul(ps(2048 + j * 128, 2048 + (j + 1) * 128),
                                 self.pt_sb[:, (c * NJ + j) * 128:
                                            (c * NJ + j + 1) * 128],
                                 self.v_tm[:, (seg * NJ + c) * 128:
                                           (seg * NJ + c + 1) * 128],
                                 start=(c == 0), stop=(c == NJ - 1))
        t_dot = self.inc(ins, s_pe)
        if self.fill_queue:
            self.fill_queue.pop(0)()

        self.wait(VEC, s_pe, t_upd)
        VEC.tensor_sub(self.mem_sb[:, h * 128:(h + 1) * 128],
                       self.mem_sb[:, h * 128:(h + 1) * 128],
                       ps(3712, 3840))
        VEC.reduce_sum(self.zsum[:], self.sk_fm[:, sbase:sbase + 512],
                       axis=AX)
        VEC.drain()
        VEC.tensor_add(self.z_sb[:, h:h + 1], self.z_sb[:, h:h + 1],
                       self.zsum[:])
        VEC.drain()
        VEC.tensor_copy(self.z_bf[:, h:h + 1], self.z_sb[:, h:h + 1])
        t_z = self.vflush()
        self.memupd_tok[h] = t_z
        self.bank_read([7], s_dve, t_z)

        self.wait(VEC, s_pe, t_dot)
        self.wait(VEC, s_pe, self.tail_trans_tok)
        for j in range(NJ):
            VEC.tensor_scalar_mul(self.adot_tm[:, j * 128:(j + 1) * 128],
                                  ps(2048 + j * 128, 2048 + (j + 1) * 128),
                                  self.recips[:, 8 + j:9 + j])
        VEC.drain()
        for j in range(NJ):
            VEC.scalar_tensor_tensor(
                self.dmd_tm[:, j * 128:(j + 1) * 128],
                ps(2560 + j * 128, 2560 + (j + 1) * 128),
                self.recips[:, 0 + j:1 + j],
                self.adot_tm[:, j * 128:(j + 1) * 128],
                ALU.mult, ALU.subtract)
        t_post = self.vflush()
        self.bank_read([4], s_dve, t_post)
        self.bank_read([5], s_dve, t_post)

        def tail():
            _, t_adfm = self.transpose_batch(
                [self.adot_tm[:, j * 128:(j + 1) * 128] for j in range(NJ)],
                2048, dst_ap=self.adot_fm[:, 0:512],
                tok_req=(s_dve, t_post))
            t_dmdp, _ = self.transpose_batch(
                [self.dmd_tm[:, j * 128:(j + 1) * 128] for j in range(NJ)],
                2304, tok_req=(s_dve, t_post))
            self.tail_trans_tok = t_dmdp
            self.wait(VEC, s_pe, t_dmdp)
            self.wait(VEC, s_act, t_adfm)
            self.wait(VEC, s_act, self.t_beta_sig)
            self.wait(VEC, s_st, st_hi)
            H = self.cfg.H
            U = 128 // H
            blkw = 512 // H
            dmd_fm = self.ps_bf(2304, 512)
            in0 = dmd_fm.rearrange("p (j u w) -> p w j u", j=NJ, u=U, w=H)
            in1 = self.adot_fm.ap().rearrange("p (j u w) -> p w j u", j=NJ,
                                              u=U, w=H)
            outv = self.att_all.rearrange("p (w x) -> p w x", w=H) \
                [:, :, sbase + h * blkw: sbase + (h + 1) * blkw] \
                .rearrange("p w (j u) -> p w j u", j=NJ)
            t_att = self.inc(VEC.scalar_tensor_tensor(
                outv, in0, self.beta_sb[:, h:h + 1], in1, ALU.mult, ALU.add),
                s_dve)
            self.att_tok_last = t_att
            self.bank_read([4], s_dve, t_att)
        return tail

    def build(self):
        self.declare_io()
        self.declare_mem()
        blk = self.nc.Block()
        blk.__enter__()

        @blk.gpsimd
        def _(g):
            self.emit()
        blk.__exit__(None, None, None)
        return self.nc


def _rearrange_weights(cfg, inp):
    NK, H, E = cfg.NK, cfg.H, cfg.E
    FEC, FSHC, NDQ = cfg.FEC, cfg.FSHC, cfg.NDQ

    def colblocks(w):
        m = w.shape[1] // 128
        return np.ascontiguousarray(
            w.reshape(NK, 128, m, 128).transpose(2, 1, 0, 3)
            .reshape(m, 128, NK * 128))

    out = {}
    out["wq_r"] = colblocks(inp["L0_attn_Wq"])
    out["wk_r"] = colblocks(inp["L0_attn_Wk"])
    out["wv_r"] = colblocks(inp["L0_attn_Wv"])
    out["wo_r"] = colblocks(inp["L0_attn_Wo"])
    out["betas_r"] = np.ascontiguousarray(
        np.asarray(inp["L0_attn_betas"]).reshape(H, 128).T)
    out["gate_r"] = np.ascontiguousarray(
        inp["L0_ffn_gate_w"].reshape(NK, 128, E).transpose(1, 0, 2)
        .reshape(128, NK * E))
    out["wg_r"] = np.ascontiguousarray(
        inp["L0_ffn_wg"].reshape(E, NK, 128, FEC, 128)
        .transpose(0, 3, 2, 1, 4).reshape(E, FEC, 128, NK * 128))
    out["wu_r"] = np.ascontiguousarray(
        inp["L0_ffn_wu"].reshape(E, NK, 128, FEC, 128)
        .transpose(0, 3, 2, 1, 4).reshape(E, FEC, 128, NK * 128))
    out["wd_r"] = np.ascontiguousarray(
        inp["L0_ffn_wd"].reshape(E, FEC, 128, NDQ, 512)
        .transpose(0, 3, 2, 1, 4).reshape(E, NDQ, 128, FEC * 512))
    out["sg_r"] = np.ascontiguousarray(
        inp["L0_ffn_sg"].reshape(NK, 128, FSHC, 128).transpose(2, 1, 0, 3)
        .reshape(FSHC, 128, NK * 128))
    out["su_r"] = np.ascontiguousarray(
        inp["L0_ffn_su"].reshape(NK, 128, FSHC, 128).transpose(2, 1, 0, 3)
        .reshape(FSHC, 128, NK * 128))
    out["sd_r"] = np.ascontiguousarray(
        inp["L0_ffn_sd"].reshape(FSHC, 128, NDQ, 512).transpose(2, 1, 0, 3)
        .reshape(NDQ, 128, FSHC * 512))
    return out


def kernel(**inputs):
    cfg = Cfg()
    inputs = {k: np.asarray(v) for k, v in inputs.items()}
    x = np.ascontiguousarray(inputs["L0_attn_x"])
    w = _rearrange_weights(cfg, inputs)
    nc = Builder(cfg).build()
    in_maps = []
    for c in range(8):
        m = dict(w)
        m["x"] = np.ascontiguousarray(x[c * cfg.B:(c + 1) * cfg.B])
        in_maps.append(m)
    res = run_bass_kernel_spmd(nc, in_maps, core_ids=list(range(8)))
    return np.concatenate([r["y"] for r in res.results], axis=0)


# version 06 (final)
# speedup vs baseline: 1.0702x; 1.0702x over previous
import sys
for _p in ("/opt/trn_rl_repo",):
    if _p not in sys.path:
        sys.path.insert(0, _p)

import numpy as np
import ml_dtypes
import concourse.bass as bass
import concourse.mybir as mybir
from concourse.bass_utils import run_bass_kernel_spmd

F32 = mybir.dt.float32
BF16 = mybir.dt.bfloat16
AF = mybir.ActivationFunctionType
ALU = mybir.AluOpType
AX = mybir.AxisListType.X

NEG_BIG = -1.0e30
INV_SQRT_DK = 1.0 / (128.0 ** 0.5)
LN_EPS = 1.0e-5
N_CB = 3
N_RB = 2


def banks_of(c0, c1):
    return list(range(c0 // 512, (c1 + 511) // 512))


class Cfg:
    def __init__(self, D=2048, S=2048, B=2, E=32, FE=768, TOPK=6, FSH=1536,
                 dbg=False, stop="full", nheads=None):
        self.D, self.S, self.B, self.E = D, S, B, E
        self.FE, self.TOPK, self.FSH = FE, TOPK, FSH
        self.NK = D // 128
        self.H = D // 128
        self.SEG = 512
        self.NJ = self.SEG // 128
        self.T = 1024
        self.NTC = self.T // 128
        self.NGRP = S // self.T
        self.FEC = FE // 128
        self.FSHC = FSH // 128
        self.NDQ = max(1, D // 512)
        self.dbg = dbg
        self.stop = stop
        self.nheads = nheads
        self.nexp = None
        assert self.FSHC % 2 == 0 and S % self.T == 0 and D % 512 == 0


class Sem:
    __slots__ = ("h", "c", "name")

    def __init__(self, nc, name):
        self.h = nc.alloc_semaphore(name)
        self.c = 0
        self.name = name


class Builder:
    def __init__(self, cfg: Cfg):
        self.cfg = cfg
        self.nc = bass.Bass("TRN2", target_bir_lowering=False,
                    dynamic_dma_scratch_size=16384)
        self._memo = {}
        self.bank_cons = [dict() for _ in range(8)]
        self.dbg_outs = set()

    def sem(self, name):
        return Sem(self.nc, name)

    def inc(self, ins, sem, n=1):
        ins.then_inc(sem.h, n)
        sem.c += n
        return sem.c

    def wait(self, eng, sem, tok):
        if sem is None or tok is None or tok <= 0:
            return
        if sem is self.s_xq or sem is self.s_st:
            tok = sem.c
        assert tok <= sem.c, f"wait on future token {sem.name}:{tok}>{sem.c}"
        key = (id(eng), sem.name)
        if self._memo.get(key, 0) >= tok:
            return
        eng.wait_ge(sem.h, tok)
        self._memo[key] = tok

    def bank_read(self, banks, sem, tok):
        for b in banks:
            d = self.bank_cons[b]
            if d.get(sem, 0) < tok:
                d[sem] = tok

    def pe_claim(self, banks):
        for b in banks:
            d = self.bank_cons[b]
            for sem, tok in d.items():
                self.wait(self.nc.tensor, sem, tok)
            d.clear()

    def declare_io(self):
        cfg, nc = self.cfg, self.nc
        D, NK, H, E = cfg.D, cfg.NK, cfg.H, cfg.E
        dt = nc.dram_tensor
        self.x_in = dt("x", [cfg.B, cfg.S, D], F32, kind="ExternalInput").ap()
        self.wq_r = dt("wq_r", [H, 128, NK * 128], F32, kind="ExternalInput").ap()
        self.wk_r = dt("wk_r", [H, 128, NK * 128], F32, kind="ExternalInput").ap()
        self.wv_r = dt("wv_r", [H, 128, NK * 128], F32, kind="ExternalInput").ap()
        self.wo_r = dt("wo_r", [NK, 128, NK * 128], F32, kind="ExternalInput").ap()
        self.betas_r = dt("betas_r", [128, H], F32, kind="ExternalInput").ap()
        self.gate_r = dt("gate_r", [128, NK * E], F32, kind="ExternalInput").ap()
        self.wg_r = dt("wg_r", [E, cfg.FEC, 128, NK * 128], F32,
                       kind="ExternalInput").ap()
        self.wu_r = dt("wu_r", [E, cfg.FEC, 128, NK * 128], F32,
                       kind="ExternalInput").ap()
        self.wd_r = dt("wd_r", [E, cfg.NDQ, 128, cfg.FEC * 512], F32,
                       kind="ExternalInput").ap()
        self.sg_r = dt("sg_r", [cfg.FSHC, 128, NK * 128], F32,
                       kind="ExternalInput").ap()
        self.su_r = dt("su_r", [cfg.FSHC, 128, NK * 128], F32,
                       kind="ExternalInput").ap()
        self.sd_r = dt("sd_r", [cfg.NDQ, 128, cfg.FSHC * 512], F32,
                       kind="ExternalInput").ap()
        self.y_out = dt("y", [cfg.B, cfg.S, D], F32, kind="ExternalOutput").ap()

        ident_np = np.eye(128, dtype=ml_dtypes.bfloat16)
        self.ident_d = nc.inline_tensor(ident_np, name="c_ident")
        NJ = cfg.NJ
        mw = 512 + (NJ - 1) * 128
        p = np.arange(128)[:, None]
        c = np.arange(mw)[None, :]
        mask_np = np.where(c <= (NJ - 1) * 128 + p, 1.0, 0.0)
        self.mask_d = nc.inline_tensor(
            mask_np.astype(ml_dtypes.bfloat16), name="c_mask")

    def declare_mem(self):
        cfg, nc = self.cfg, self.nc
        D, NK, H, T, NTC = cfg.D, cfg.NK, cfg.H, cfg.T, cfg.NTC
        sb = nc.alloc_sbuf_tensor
        self.beta_sb = sb("beta_sb", [128, H], F32)
        self.mem_sb = sb("mem_sb", [128, H * 128], F32)
        self.z_sb = sb("z_sb", [128, H], F32)
        self.z_bf = sb("z_bf", [128, H], BF16)
        self.membf = sb("membf", [128, 128], BF16)
        self.gate_sb = sb("gate_sb", [128, NK * cfg.E], BF16)
        self.ident_sb = sb("ident_sb", [128, 128], BF16)
        self.mask_sb = sb("mask_sb", [128, 512 + (cfg.NJ - 1) * 128], BF16)
        self.ln_attn = sb("ln_attn", [128, NTC * D], BF16)
        ra = sb("ra", [128, NK * T], F32)
        self.acc = ra.ap()
        half = (NK * T) // 2
        self.x_ln_fm = ra.ap()[:, 0:half].bitcast(BF16)
        self.att_all = ra.ap()[:, half:2 * half].bitcast(BF16)
        self.ao_tm = self.x_ln_fm
        self.stage = [ra.ap()[:, half + i * D: half + (i + 1) * D]
                      for i in range(2)]
        self.x1_ln_fm = sb("x1_ln_fm", [128, NK * T], BF16)
        self.cb_ring = sb("cb_ring", [128, N_CB * NK * 128], BF16)
        self.rbw = max(cfg.FEC, cfg.FSHC // 2) * 512
        self.rb_ring = sb("rb_ring", [128, N_RB * self.rbw], BF16)
        self.q_db = self.rb_ring.ap()[:, 0:T]
        self.k_db = self.rb_ring.ap()[:, T:2 * T]
        actw = max(cfg.FEC, cfg.FSHC // 2) * T
        self.act_e = sb("act_e", [128, actw], BF16)
        self.act_sh = self.act_e
        self.silu_s = sb("silu_s", [128, T], BF16)
        self.combine = sb("combine", [128, NTC * cfg.E], F32)
        self.q_fm = sb("q_fm", [128, T], BF16)
        self.k_fm = sb("k_fm", [128, T], BF16)

        self.sq_fm = sb("sq_fm", [128, T], BF16)
        self.sk_fm = sb("sk_fm", [128, T], BF16)
        self.v_tm = sb("v_tm", [128, T], BF16)
        self.sk_tm = sb("sk_tm", [128, T], BF16)
        scr4 = sb("scr4", [128, 2 * T], BF16)
        self.e_t1 = scr4.ap()[:, 0:T]
        self.e_t2 = scr4.ap()[:, T:2 * T]
        self.pt_sb = scr4.ap()[:, 0:cfg.NJ * 512]
        self.ao_fm_s = scr4.ap()[:, 0:T]
        self.p_sb = sb("p_sb", [128, cfg.NJ * 512], BF16)
        self.v_fm = self.p_sb.ap()[:, 0:T]
        self.adot_tm = sb("adot_tm", [128, 512], BF16)
        self.adot_fm = sb("adot_fm", [128, 512], BF16)
        self.dmd_tm = sb("dmd_tm", [128, 512], BF16)
        self.negdel = sb("negdel", [128, 512], BF16)
        self.recips = sb("recips", [128, 16], F32)
        self.sumrow = sb("sumrow", [128, cfg.NJ], F32)
        self.negm = sb("negm", [128, cfg.NJ], F32)

        self.zsum = sb("zsum", [128, 1], F32)
        self.bn24 = sb("bn24", [128, 24], F32)
        self.mv2 = sb("mv2", [128, 2], F32)
        self.std1 = sb("std1", [128, 1], F32)
        self.negmr = sb("negmr", [128, 1], F32)
        self.eps_sb = sb("eps_sb", [128, 1], F32)
        self.tick_v = sb("tick_v", [1, 4], F32)
        self.tick_s = sb("tick_s", [1, 4], F32)
        self.rstd1 = sb("rstd1", [128, 1], F32)
        self.xln_tm = sb("xln_tm", [128, D], BF16)
        self.top8 = sb("top8", [128, 8], F32)
        self.gsc = sb("gsc", [128, max(cfg.E, cfg.H)], F32)
        self.gsel = sb("gsel", [128, cfg.E], F32)
        self.gsum = sb("gsum", [128, 1], F32)
        self.grcp = sb("grcp", [128, 1], F32)

        self.psum = nc.alloc_psum_tensor("psum", [128, 4096], F32)

        self.s_pe = self.sem("s_pe")
        self.s_dve = self.sem("s_dve")
        self.s_act = self.sem("s_act")
        self.s_cb = [self.sem(f"s_cb{i}") for i in range(N_CB)]
        self.s_rb = [self.sem(f"s_rb{i}") for i in range(N_RB)]
        self.s_wq = self.sem("s_wq")
        self.s_xq = self.sem("s_xq")
        self.s_st = self.sem("s_st")

        self.cb_free = [(None, 0)] * N_CB
        self.rb_free = [(None, 0)] * N_RB
        self.cb_next = 0

        self.acc_store_tok = [0] * cfg.NTC
        self.lnattn_tok = [0] * cfg.NTC
        self.attall_lastuse_pe = 0
        self.xlnfm_lastuse_pe = 0
        self.x1lnfm_lastuse_pe = 0
        self.xln_slot_free = [(None, 0), (None, 0)]
        self.stage_slot_free = [(None, 0), (None, 0)]
        self.memupd_tok = [0] * cfg.H
        self.membf_lastuse_pe = 0
        self.last_membf_act = 0
        self.psb_lastuse_pe = 0
        self.combine_tok = [0] * cfg.NTC
        self.acc_ready_tok = [(None, 0)] * cfg.NTC
        self.prev_head_dve = 0
        self.att_tok_last = 0
        self.att_tail = None
        self.tail_trans_tok = 0
        self.fill_queue = []
        self.qk_evac = {}
        self.qk_lastuse_pe = 0

    def ps(self, c0, c1):
        return self.psum.ap()[:, c0:c1]

    def ps_bf(self, c0, nbf):
        return self.psum.ap()[:, c0:c0 + (nbf + 1) // 2].bitcast(BF16)

    def cb(self, i):
        w = self.cfg.NK * 128
        return self.cb_ring.ap()[:, i * w:(i + 1) * w]

    def rb(self, i):
        return self.rb_ring.ap()[:, i * self.rbw:(i + 1) * self.rbw]

    def dbg_tap(self, name, src_ap, tok):
        if not self.cfg.dbg or name in self.dbg_outs:
            return
        self.dbg_outs.add(name)
        nc, SYN = self.nc, self.nc.sync
        d = nc.dram_tensor("dbg_" + name, list(src_ap.shape), src_ap.dtype,
                           kind="ExternalOutput").ap()
        sem, t = tok
        self.wait(SYN, sem, t)
        self.inc(SYN.dma_start(out=d, in_=src_ap), self.s_st, 16)

    def xdma(self, out_ap, in_ap, sem):
        SYN = self.nc.sync
        if sem.c > 0:
            self.wait(SYN, sem, sem.c)
        return self.inc(SYN.dma_start(out=out_ap, in_=in_ap), sem, 16)

    def load_cb(self, src_ap):
        POO = self.nc.gpsimd
        i = self.cb_next
        self.cb_next = (i + 1) % N_CB
        sem, tok = self.cb_free[i]
        self.wait(POO, sem, tok)
        rdy = self.inc(POO.dma_start(out=self.cb(i), in_=src_ap),
                       self.s_cb[i], 16)
        return i, (self.s_cb[i], rdy)

    def load_rb(self, i, src_ap, width):
        POO = self.nc.gpsimd
        sem, tok = self.rb_free[i]
        self.wait(POO, sem, tok)
        rdy = self.inc(POO.dma_start(out=self.rb(i)[:, 0:width], in_=src_ap),
                       self.s_rb[i], 16)
        return (self.s_rb[i], rdy)

    def transpose_batch(self, srcs, bank_c0, dst_ap=None, tok_req=None,
                        act_pre=None):
        TEN, SCA = self.nc.tensor, self.nc.scalar
        n = len(srcs)
        assert n <= 8
        bks = banks_of(bank_c0, bank_c0 + 64 * n)
        self.pe_claim(bks)
        self.wait(TEN, self.s_xq, self.t_ident)
        if tok_req is not None:
            self.wait(TEN, *tok_req)
        ins = None
        for i, s in enumerate(srcs):
            ins = TEN.transpose(self.ps_bf(bank_c0 + 64 * i, 128), s,
                                self.ident_sb[:])
        ptok = self.inc(ins, self.s_pe)
        atok = None
        if dst_ap is not None:
            if act_pre is not None:
                self.wait(SCA, *act_pre)
            self.wait(SCA, self.s_pe, ptok)
            atok = self.inc(SCA.copy(dst_ap, self.ps_bf(bank_c0, 128 * n)),
                            self.s_act)
            self.bank_read(bks, self.s_act, atok)
        return ptok, atok

    def vflush(self):
        VEC = self.nc.vector
        VEC.drain()
        return self.inc(VEC.memset(self.tick_v[:], 0.0), self.s_dve)

    def sflush(self):
        SCA = self.nc.scalar
        SCA.drain()
        return self.inc(SCA.memzero(self.tick_s[:]), self.s_act)

    def ln_stats(self, x_ap, nch):
        VEC, SCA = self.nc.vector, self.nc.scalar
        for c in range(nch):
            VEC.bn_stats(self.bn24[:, 6 * c:6 * (c + 1)],
                         x_ap[:, 512 * c:512 * (c + 1)])
        VEC.drain()
        t = self.inc(VEC.bn_aggr(self.mv2[:], self.bn24[:, 0:6 * nch]),
                     self.s_dve)
        self.wait(SCA, self.s_dve, t)
        t = self.inc(SCA.activation(self.std1[:], self.mv2[:, 1:2], AF.Sqrt,
                                    bias=self.eps_sb[:]), self.s_act)
        self.wait(VEC, self.s_act, t)
        VEC.reciprocal(self.rstd1[:], self.std1[:])
        VEC.drain()
        VEC.tensor_scalar(self.negmr[:], self.mv2[:, 0:1], self.rstd1[:],
                          -1.0, ALU.mult, ALU.mult)
        t = self.vflush()
        return t

    def ln_apply(self, out_ap, in_ap, t_stats):
        SCA = self.nc.scalar
        self.wait(SCA, self.s_dve, t_stats)
        return self.inc(SCA.activation(out_ap, in_ap, AF.Identity,
                                       bias=self.negmr[:],
                                       scale=self.rstd1[:]), self.s_act)

    def emit(self):
        cfg, nc = self.cfg, self.nc
        VEC, SCA, SYN, POO = nc.vector, nc.scalar, nc.sync, nc.gpsimd
        self.t_ident = self.xdma(self.ident_sb[:], self.ident_d.ap(),
                                 self.s_xq)
        self.t_mask = self.xdma(self.mask_sb[:], self.mask_d.ap(), self.s_xq)
        t_beta = self.xdma(self.gsc[:, 0:cfg.H], self.betas_r, self.s_xq)
        self.t_gate = self.inc(POO.dma_start(out=self.gate_sb[:],
                                             in_=self.gate_r), self.s_wq, 16)
        VEC.memset(self.eps_sb[:], LN_EPS)
        self.wait(SCA, self.s_xq, t_beta)
        self.t_beta_sig = self.inc(
            SCA.activation(self.beta_sb[:], self.gsc[:, 0:cfg.H],
                           AF.Sigmoid), self.s_act)

        for q in range(cfg.B):
            self.wait(VEC, self.s_act, self.last_membf_act)
            self.wait(VEC, self.s_pe, self.membf_lastuse_pe)
            VEC.memset(self.mem_sb[:], 0.0)
            VEC.memset(self.z_sb[:], 1.0 / 128.0)
            t0 = self.inc(VEC.memset(self.z_bf[:], 1.0 / 128.0), self.s_dve)
            for h in range(cfg.H):
                self.memupd_tok[h] = t0
            for g in range(cfg.NGRP):
                self.emit_group(q, g)
        SYN.wait_ge(self.s_st.h, self.s_st.c)

    def emit_group(self, q, g):
        cfg, nc = self.cfg, self.nc
        TEN, VEC, SCA = nc.tensor, nc.vector, nc.scalar
        SYN, POO = nc.sync, nc.gpsimd
        D, NK, H, T, NTC, NJ, E = (cfg.D, cfg.NK, cfg.H, cfg.T, cfg.NTC,
                                   cfg.NJ, cfg.E)
        FEC, FSHC, NDQ = cfg.FEC, cfg.FSHC, cfg.NDQ
        s_pe, s_dve, s_act = self.s_pe, self.s_dve, self.s_act
        s_wq, s_xq, s_st = self.s_wq, self.s_xq, self.s_st
        ps = self.ps
        tok0 = g * T
        nhalf = (NK + 7) // 8

        st_lo = max(self.acc_store_tok[:NTC // 2])
        for tc in range(NTC):
            sl = tc % 2
            xsl = 0
            stg = self.stage[sl]
            self.wait(SYN, s_st, self.acc_store_tok[NTC // 2 + sl])
            self.wait(SYN, s_pe, self.attall_lastuse_pe)
            sem, tkf = self.stage_slot_free[sl]
            self.wait(SYN, sem, tkf)
            xin = self.x_in[q, tok0 + tc * 128: tok0 + (tc + 1) * 128, :]
            t_x = self.xdma(stg, xin, s_xq)
            self.wait(VEC, s_xq, t_x)
            t_stats = self.ln_stats(stg, D // 512)
            semx, tkx = self.xln_slot_free[xsl]
            self.wait(SCA, semx, tkx)
            xlns = self.xln_tm.ap()[:, 0:D]
            ta = self.ln_apply(xlns, stg, t_stats)
            self.stage_slot_free[sl] = (s_act, ta)
            for half in range(nhalf):
                kk = min(8, NK - half * 8)
                srcs = [self.xln_tm.ap()[:, k * 128:(k + 1) * 128]
                        for k in range(half * 8, half * 8 + kk)]
                dst = self.x_ln_fm.rearrange("p (k t) -> p k t", k=NK) \
                    [:, half * 8:half * 8 + kk, tc * 128:(tc + 1) * 128]
                self.wait(TEN, s_pe, self.xlnfm_lastuse_pe)
                _, atok = self.transpose_batch(
                    srcs, 3072 if half % 2 == 0 else 3584, dst_ap=dst,
                    tok_req=(s_act, ta), act_pre=(s_st, st_lo))
                self.xln_slot_free[xsl] = (s_act, atok)
        t_xlnfm_ready = max(t for _, t in self.xln_slot_free)
        self.dbg_tap("x_ln_fm", self.x_ln_fm[:, :], (s_act, t_xlnfm_ready))
        if cfg.stop == "a1":
            return

        st_hi = max(self.acc_store_tok[NTC // 2:])
        self.att_tail = None
        self.fill_queue = []
        nH = cfg.nheads or H

        def qk_piece(hh, name, hf, c0b=None):
            wsrc = self.wq_r if name == "q" else self.wk_r
            buf = ((self.q_fm[:, :] if name == "q" else self.k_fm[:, :])
                   if hh % 2 == 0 else
                   (self.q_db if name == "q" else self.k_db))
            key = (hh, name)
            if key not in self._qk_slot:
                self._qk_slot[key] = self.load_cb(wsrc[hh])
            sl, trdy = self._qk_slot[key]
            c0 = (0 if name == "q" else 1024) if c0b is None else c0b - hf * 512
            bank = [c0 // 512 + hf]
            self.pe_claim(bank)
            self.wait(TEN, *trdy)
            self.wait(TEN, s_act, t_xlnfm_ready)
            ins = None
            for k in range(NK):
                ins = TEN.matmul(
                    ps(c0 + hf * 512, c0 + (hf + 1) * 512),
                    self.cb(sl)[:, k * 128:(k + 1) * 128],
                    self.x_ln_fm[:, k * T + hf * 512: k * T + (hf + 1) * 512],
                    start=(k == 0), stop=(k == NK - 1))
            ptok = self.inc(ins, s_pe)
            self.cb_free[sl] = (s_pe, ptok)
            self.xlnfm_lastuse_pe = ptok
            self.wait(SCA, s_dve, self.prev_head_dve)
            self.wait(SCA, s_pe, ptok)
            etok = self.inc(SCA.copy(buf[:, hf * 512:(hf + 1) * 512],
                                     ps(c0 + hf * 512, c0 + (hf + 1) * 512)),
                            s_act)
            self.bank_read(bank, s_act, etok)
            self.qk_evac[key] = max(self.qk_evac.get(key, 0), etok)

        self._qk_slot = {}
        for h in range(nH):
            qb = self.q_fm[:, :] if h % 2 == 0 else self.q_db
            kb = self.k_fm[:, :] if h % 2 == 0 else self.k_db
            if h == 0:
                for name in ("q", "k"):
                    for hf in range(2):
                        qk_piece(0, name, hf)
            sl_v, tv = self.load_cb(self.wv_r[h])
            self.wait(SCA, s_dve, self.prev_head_dve)
            etok_v = 0
            for hf in range(2):
                bank = [hf]
                self.pe_claim(bank)
                self.wait(TEN, *tv)
                self.wait(TEN, s_act, t_xlnfm_ready)
                ins = None
                for k in range(NK):
                    ins = TEN.matmul(
                        ps(hf * 512, (hf + 1) * 512),
                        self.cb(sl_v)[:, k * 128:(k + 1) * 128],
                        self.x_ln_fm[:, k * T + hf * 512:
                                     k * T + (hf + 1) * 512],
                        start=(k == 0), stop=(k == NK - 1))
                ptok = self.inc(ins, s_pe)
                self.cb_free[sl_v] = (s_pe, ptok)
                self.xlnfm_lastuse_pe = ptok
                self.wait(SCA, s_pe, ptok)
                etok_v = self.inc(SCA.copy(self.v_fm[:, hf * 512:(hf + 1) * 512],
                                           ps(hf * 512, (hf + 1) * 512)),
                                  s_act)
                self.bank_read(bank, s_act, etok_v)
            sig_tok = {}
            for name, srcb, dst in (("q", qb, self.sq_fm),
                                    ("k", kb, self.sk_fm)):
                self.wait(VEC, s_act, self.qk_evac[(h, name)])
                t1 = self.inc(VEC.tensor_scalar_min(self.e_t1, srcb,
                                                    0.0), s_dve)
                self.wait(SCA, s_dve, t1)
                t2 = self.inc(SCA.activation(self.e_t2, self.e_t1,
                                             AF.Exp), s_act)
                self.wait(VEC, s_act, t2)
                sig_tok[name] = self.inc(VEC.scalar_tensor_tensor(
                    dst[:], srcb, 0.0, self.e_t2, ALU.max, ALU.add),
                    s_dve)
            _, t_vtm = self.transpose_batch(
                [self.v_fm[:, c * 128:(c + 1) * 128] for c in range(NTC)],
                1024, dst_ap=self.v_tm[:, 0:NTC * 128],
                tok_req=(s_act, etok_v))
            _, t_sktm = self.transpose_batch(
                [self.sk_fm[:, c * 128:(c + 1) * 128] for c in range(NTC)],
                1536, dst_ap=self.sk_tm[:, 0:NTC * 128],
                tok_req=(s_dve, sig_tok["k"]))
            if h == 0:
                self.dbg_tap("q_fm", self.q_fm[:, :],
                             (s_act, self.qk_evac[(0, "q")]))
                self.dbg_tap("sq_fm", self.sq_fm[:, :], (s_dve, sig_tok["q"]))
                self.dbg_tap("v_tm", self.v_tm[:, :], (s_act, t_vtm))
            if h + 1 < nH:
                for name in ("q", "k"):
                    for hf in range(2):
                        self.fill_queue.append(
                            (lambda hh=h + 1, n=name, f=hf, c0b=None:
                             qk_piece(hh, n, f, c0b)))
            for seg in range(2):
                self.att_tail = self.emit_attn_core(
                    h, seg, qb, kb, sig_tok, t_vtm, t_sktm, st_hi,
                    filler=self.att_tail)
            self.prev_head_dve = self.memupd_tok[h]
            for p in self.fill_queue:
                p()
            self.fill_queue = []
        if self.att_tail is not None:
            self.att_tail()
            self.att_tail = None
        self.qk_lastuse_pe = self.s_pe.c
        self.rb_free = [(self.s_pe, max(t for _, t in
                                        [(None, self.qk_lastuse_pe)] +
                                        [f for f in self.rb_free]))] * N_RB
        self.dbg_tap("att_all", self.att_all[:, :],
                     (self.s_dve, self.att_tok_last))
        if cfg.stop == "heads":
            return

        t_ao = 0
        for m in range(NK):
            sl_o, to = self.load_cb(self.wo_r[m])
            c0 = 0 if (m % 2 == 0) else 1024
            bks = banks_of(c0, c0 + 1024)
            self.pe_claim(bks)
            self.wait(TEN, *to)
            self.wait(TEN, s_dve, self.att_tok_last)
            ins = None
            for k in range(NK):
                for hf in range(2):
                    ins = TEN.matmul(
                        ps(c0 + hf * 512, c0 + (hf + 1) * 512),
                        self.cb(sl_o)[:, k * 128:(k + 1) * 128],
                        self.att_all[:, k * T + hf * 512:
                                     k * T + (hf + 1) * 512],
                        start=(k == 0), stop=(k == NK - 1))
            ptok = self.inc(ins, s_pe)
            self.cb_free[sl_o] = (s_pe, ptok)
            self.attall_lastuse_pe = ptok
            self.wait(SCA, s_pe, ptok)
            etok = self.inc(SCA.copy(self.ao_fm_s, ps(c0, c0 + 1024)),
                            s_act)
            self.bank_read(bks, s_act, etok)
            srcs = [self.ao_fm_s[:, tc * 128:(tc + 1) * 128]
                    for tc in range(NTC)]
            dst = self.ao_tm.rearrange("p (t d) -> p t d", t=NTC) \
                [:, :, m * 128:(m + 1) * 128]
            _, t_ao = self.transpose_batch(srcs, 3072, dst_ap=dst,
                                           tok_req=(s_act, etok))
        self.dbg_tap("ao_tm", self.ao_tm[:, 0:NTC * D], (s_act, t_ao))

        for tc in range(NTC):
            src = self.ao_tm[:, tc * D:(tc + 1) * D]
            self.wait(VEC, s_act, t_ao)
            t_stats = self.ln_stats(src, D // 512)
            self.lnattn_tok[tc] = self.ln_apply(
                self.ln_attn[:, tc * D:(tc + 1) * D], src, t_stats)
        self.dbg_tap("ln_attn", self.ln_attn[:, :],
                     (s_act, self.lnattn_tok[NTC - 1]))
        if cfg.stop == "attn":
            return

        for tc in range(NTC):
            acc_t = self.acc[:, tc * D:(tc + 1) * D]
            if tc < NTC // 2:
                self.wait(SYN, s_act, self.lnattn_tok[2 * tc + 1])
            else:
                self.wait(SYN, s_pe, self.attall_lastuse_pe)
                if tc - NTC // 2 < 2:
                    sem, tkf = self.stage_slot_free[tc - NTC // 2]
                    self.wait(SYN, sem, tkf)
            self.wait(SYN, s_st, self.acc_store_tok[tc])
            xin = self.x_in[q, tok0 + tc * 128: tok0 + (tc + 1) * 128, :]
            t_x = self.xdma(acc_t, xin, s_xq)
            self.wait(VEC, s_xq, t_x)
            self.wait(VEC, s_act, self.lnattn_tok[tc])
            VEC.tensor_add(acc_t, acc_t, self.ln_attn[:, tc * D:(tc + 1) * D])
            t_stats = self.ln_stats(acc_t, D // 512)
            sl = 0
            semx, tkx = self.xln_slot_free[sl]
            self.wait(SCA, semx, tkx)
            xlns = self.xln_tm.ap()[:, 0:D]
            ta = self.ln_apply(xlns, acc_t, t_stats)
            for half in range(nhalf):
                kk = min(8, NK - half * 8)
                srcs = [self.xln_tm.ap()[:, k * 128:(k + 1) * 128]
                        for k in range(half * 8, half * 8 + kk)]
                dst = self.x1_ln_fm.rearrange("p (k t) -> p k t", k=NK) \
                    [:, half * 8:half * 8 + kk, tc * 128:(tc + 1) * 128]
                self.wait(TEN, s_pe, self.x1lnfm_lastuse_pe)
                _, atok = self.transpose_batch(
                    srcs, 3072 if half % 2 == 0 else 3584, dst_ap=dst,
                    tok_req=(s_act, ta))
                self.xln_slot_free[sl] = (s_act, atok)
            t_x1tc = self.xln_slot_free[sl][1]
            self.pe_claim([3])
            self.wait(TEN, s_act, t_x1tc)
            self.wait(TEN, s_wq, self.t_gate)
            ins = None
            for k in range(NK):
                ins = TEN.matmul(ps(1536, 1536 + E),
                                 self.x1_ln_fm[:, k * T + tc * 128:
                                               k * T + (tc + 1) * 128],
                                 self.gate_sb[:, k * E:(k + 1) * E],
                                 start=(k == 0), stop=(k == NK - 1))
            tg = self.inc(ins, s_pe)
            self.wait(SCA, s_pe, tg)
            SCA.activation(self.gsc[:, 0:E], ps(1536, 1536 + E), AF.Sigmoid)
            tsg = self.sflush()
            self.bank_read([3], s_act, tsg)
            self.wait(VEC, s_act, tsg)
            VEC.max(out=self.top8[:], in_=self.gsc[:, 0:E])
            VEC.drain()
            VEC.tensor_scalar(self.gsel[:], self.gsc[:, 0:E],
                              self.top8[:, cfg.TOPK - 1:cfg.TOPK], None,
                              ALU.is_ge)
            VEC.drain()
            VEC.tensor_mul(self.gsel[:], self.gsel[:], self.gsc[:, 0:E])
            VEC.drain()
            VEC.reduce_sum(self.gsum[:], self.gsel[:], axis=AX)
            VEC.drain()
            VEC.tensor_scalar(self.gsum[:], self.gsum[:], 1.0e-20, None,
                              ALU.add)
            VEC.drain()
            VEC.reciprocal(self.grcp[:], self.gsum[:])
            VEC.drain()
            self.combine_tok[tc] = self.inc(VEC.tensor_scalar(
                self.combine[:, tc * E:(tc + 1) * E], self.gsel[:],
                self.grcp[:], None, ALU.mult), s_dve)
        t_x1_ready = max(t for _, t in self.xln_slot_free)
        self.dbg_tap("x1_ln_fm", self.x1_ln_fm[:, :], (s_act, t_x1_ready))
        self.dbg_tap("combine", self.combine[:, :],
                     (s_dve, self.combine_tok[NTC - 1]))
        if cfg.stop == "b1":
            return

        for half in range(2):
            nfc = FSHC // 2
            gu_tok = [0] * nfc
            for fl in range(nfc):
                fc = half * nfc + fl
                sl_g, tg_ = self.load_cb(self.sg_r[fc])
                sl_u, tu_ = self.load_cb(self.su_r[fc])
                gu_tok[fl] = self.emit_gu_act(sl_g, tg_, sl_u, tu_,
                                              t_x1_ready, self.act_sh, fl)
            self.emit_down(
                nfc, self.act_sh, gu_tok,
                lambda dq, hh=half, n=nfc: self.sd_r[dq][:, hh * n * 512:
                                                         (hh + 1) * n * 512],
                nfc * 512, first_write=(half == 0), comb_e=None)
        if cfg.stop == "shared":
            sem, t_acc = self.acc_ready_tok[0]
            self.dbg_tap("accsh", self.acc[:, 0:D], (sem, t_acc))
            return
        nE = cfg.nexp or E
        for e in range(nE):
            gu_tok = [0] * FEC
            for fc in range(FEC):
                sl_g, tg_ = self.load_cb(self.wg_r[e, fc])
                sl_u, tu_ = self.load_cb(self.wu_r[e, fc])
                gu_tok[fc] = self.emit_gu_act(sl_g, tg_, sl_u, tu_,
                                              t_x1_ready, self.act_e, fc)
            if e == 0:
                self.dbg_tap("act_e0", self.act_e[:, :],
                             (self.s_dve, gu_tok[FEC - 1]))
            self.emit_down(FEC, self.act_e, gu_tok,
                           lambda dq, e=e: self.wd_r[e, dq],
                           FEC * 512, first_write=False, comb_e=e)
        self.dbg_tap("acc0", self.acc[:, 0:D], self.acc_ready_tok[0])
        if cfg.stop == "moe":
            return

        for tc in range(NTC):
            acc_t = self.acc[:, tc * D:(tc + 1) * D]
            sem, t_acc = self.acc_ready_tok[tc]
            self.wait(VEC, sem, t_acc)
            t_stats = self.ln_stats(acc_t, D // 512)
            sl = 0
            semx, tkx = self.xln_slot_free[sl]
            self.wait(SCA, semx, tkx)
            xlns = self.xln_tm.ap()[:, 0:D]
            ta = self.ln_apply(xlns, acc_t, t_stats)
            self.wait(SYN, s_act, ta)
            xin = self.x_in[q, tok0 + tc * 128: tok0 + (tc + 1) * 128, :]
            t_x = self.xdma(acc_t, xin, s_xq)
            self.wait(VEC, s_xq, t_x)
            self.wait(VEC, s_act, ta)
            VEC.tensor_add(acc_t, acc_t, self.ln_attn[:, tc * D:(tc + 1) * D])
            tout = self.inc(VEC.tensor_add(acc_t, acc_t, xlns), s_dve)
            self.xln_slot_free[sl] = (s_dve, tout)
            self.wait(SYN, s_dve, tout)
            yout = self.y_out[q, tok0 + tc * 128: tok0 + (tc + 1) * 128, :]
            self.acc_store_tok[tc] = self.xdma(yout, acc_t, s_st)
            self.acc_ready_tok[tc] = (None, 0)

    def emit_gu_act(self, sl_g, tg, sl_u, tu, t_x1, act_buf, fslot):
        cfg, nc = self.cfg, self.nc
        TEN, VEC, SCA = nc.tensor, nc.vector, nc.scalar
        NK, T = cfg.NK, cfg.T
        ps = self.ps
        s_pe, s_dve, s_act = self.s_pe, self.s_dve, self.s_act
        toks = {}
        for c0, sl, trdy in ((0, sl_g, tg), (1024, sl_u, tu)):
            bks = banks_of(c0, c0 + 1024)
            self.pe_claim(bks)
            self.wait(TEN, *trdy)
            self.wait(TEN, s_act, t_x1)
            ins = None
            for k in range(NK):
                for hf in range(2):
                    ins = TEN.matmul(
                        ps(c0 + hf * 512, c0 + (hf + 1) * 512),
                        self.cb(sl)[:, k * 128:(k + 1) * 128],
                        self.x1_ln_fm[:, k * T + hf * 512:
                                      k * T + (hf + 1) * 512],
                        start=(k == 0), stop=(k == NK - 1))
            ptok = self.inc(ins, s_pe)
            self.cb_free[sl] = (s_pe, ptok)
            self.x1lnfm_lastuse_pe = ptok
            toks[c0] = ptok
        self.wait(SCA, s_pe, toks[0])
        t_silu = self.inc(SCA.activation(self.silu_s[:], ps(0, 1024),
                                         AF.Silu), s_act)
        self.bank_read([0, 1], s_act, t_silu)
        self.wait(VEC, s_act, t_silu)
        self.wait(VEC, s_pe, toks[1024])
        t_act = self.inc(VEC.tensor_mul(
            act_buf[:, fslot * T:(fslot + 1) * T], self.silu_s[:],
            ps(1024, 2048)), s_dve)
        self.bank_read([2, 3], s_dve, t_act)
        return t_act

    def emit_down(self, nfc, act_buf, gu_tok, src_fn, width, first_write,
                  comb_e):
        cfg, nc = self.cfg, self.nc
        TEN, VEC = nc.tensor, nc.vector
        NTC, T, D, NDQ, E = cfg.NTC, cfg.T, cfg.D, cfg.NDQ, cfg.E
        ps = self.ps
        s_pe, s_dve = self.s_pe, self.s_dve
        for dq in range(NDQ):
            slot = dq % N_RB
            trb = self.load_rb(slot, src_fn(dq), width)
            for tc in range(NTC):
                self.pe_claim([tc])
                self.wait(TEN, *trb)
                ins = None
                for fl in range(nfc):
                    self.wait(TEN, s_dve, gu_tok[fl])
                    ins = TEN.matmul(
                        ps(tc * 512, (tc + 1) * 512),
                        act_buf[:, fl * T + tc * 128:
                                fl * T + (tc + 1) * 128],
                        self.rb(slot)[:, fl * 512:(fl + 1) * 512],
                        start=(fl == 0), stop=(fl == nfc - 1))
                ptok = self.inc(ins, s_pe)
                self.rb_free[slot] = (s_pe, ptok)
                self.wait(VEC, s_pe, ptok)
                dst = self.acc[:, tc * D + dq * 512: tc * D + (dq + 1) * 512]
                if first_write:
                    ins = VEC.tensor_copy(dst, ps(tc * 512, (tc + 1) * 512))
                elif comb_e is None:
                    ins = VEC.tensor_add(dst, dst,
                                         ps(tc * 512, (tc + 1) * 512))
                else:
                    self.wait(VEC, s_dve, self.combine_tok[tc])
                    ins = VEC.scalar_tensor_tensor(
                        dst, ps(tc * 512, (tc + 1) * 512),
                        self.combine[:, tc * E + comb_e: tc * E + comb_e + 1],
                        dst, ALU.mult, ALU.add)
                etok = self.inc(ins, s_dve)
                self.bank_read([tc], s_dve, etok)
                self.acc_ready_tok[tc] = (s_dve, etok)

    def emit_attn_core(self, h, seg, qb, kb, sig_tok, t_vtm, t_sktm, st_hi,
                       filler=None):
        cfg, nc = self.cfg, self.nc
        TEN, VEC, SCA = nc.tensor, nc.vector, nc.scalar
        NJ, T = cfg.NJ, cfg.T
        ps = self.ps
        s_pe, s_dve, s_act, s_st = self.s_pe, self.s_dve, self.s_act, self.s_st
        sbase = seg * 512

        self.wait(SCA, s_dve, self.memupd_tok[h])
        self.wait(SCA, s_pe, self.membf_lastuse_pe)
        SCA.copy(self.membf[:], self.mem_sb[:, h * 128:(h + 1) * 128])
        t_mbf = self.sflush()
        self.last_membf_act = t_mbf

        self.pe_claim([0, 1, 2, 3])
        ins = None
        for j in range(NJ):
            ins = TEN.matmul(ps(j * 512, (j + 1) * 512),
                             qb[:, sbase + j * 128: sbase + (j + 1) * 128],
                             kb[:, sbase:sbase + 512],
                             start=True, stop=True)
        t_sc = self.inc(ins, s_pe)

        self.pe_claim([5, 6, 7])
        self.wait(TEN, s_act, t_mbf)
        self.wait(TEN, s_dve, sig_tok["q"])
        self.wait(TEN, s_dve, sig_tok["k"])
        for j in range(NJ):
            TEN.matmul(ps(2560 + j * 128, 2560 + (j + 1) * 128),
                       self.sq_fm[:, sbase + j * 128: sbase + (j + 1) * 128],
                       self.membf[:], start=True, stop=True)
            TEN.matmul(ps(3072 + j * 128, 3072 + (j + 1) * 128),
                       self.sk_fm[:, sbase + j * 128: sbase + (j + 1) * 128],
                       self.membf[:], start=True, stop=True)
            TEN.matmul(ps(3584 + j, 3585 + j),
                       self.sq_fm[:, sbase + j * 128: sbase + (j + 1) * 128],
                       self.z_bf[:, h:h + 1], start=True, stop=True)
            ins = TEN.matmul(ps(3588 + j, 3589 + j),
                             self.sk_fm[:, sbase + j * 128:
                                        sbase + (j + 1) * 128],
                             self.z_bf[:, h:h + 1], start=True, stop=True)
        t_sig = self.inc(ins, s_pe)
        self.membf_lastuse_pe = t_sig

        if filler is not None:
            filler()
        if self.fill_queue:
            self.fill_queue.pop(0)(c0b=2048)

        self.wait(VEC, s_pe, t_sc)
        self.wait(VEC, self.s_xq, self.t_mask)
        for j in range(NJ):
            VEC.reduce_max(self.negm[:, j:j + 1],
                           ps(j * 512, j * 512 + 128 * (j + 1)),
                           axis=AX, negate=True)
        VEC.drain()
        VEC.tensor_scalar_mul(self.negm[:, 0:NJ], self.negm[:, 0:NJ],
                              INV_SQRT_DK)
        smj = self.vflush()

        exp_tok = [0] * NJ
        for j in range(NJ):
            self.wait(SCA, s_dve, smj)
            self.wait(SCA, s_pe, self.psb_lastuse_pe)
            exp_tok[j] = self.inc(SCA.activation(
                self.p_sb[:, j * 512:(j + 1) * 512], ps(j * 512, (j + 1) * 512),
                AF.Exp, scale=INV_SQRT_DK, bias=self.negm[:, j:j + 1]), s_act)
        self.bank_read([0, 1, 2, 3], s_act, exp_tok[NJ - 1])

        for j in range(NJ):
            self.wait(VEC, s_act, exp_tok[j])
            moff = (NJ - 1 - j) * 128
            VEC.scalar_tensor_tensor(
                self.p_sb[:, j * 512:(j + 1) * 512],
                self.p_sb[:, j * 512:(j + 1) * 512], 1.0,
                self.mask_sb[:, moff:moff + 512], ALU.mult, ALU.mult,
                accum_out=self.sumrow[:, j:j + 1])
        t_pm = self.vflush()

        self.wait(VEC, s_pe, t_sig)
        VEC.reciprocal(self.recips[:, 0:NJ], ps(3584, 3584 + NJ))
        VEC.reciprocal(self.recips[:, 4:4 + NJ], ps(3588, 3588 + NJ))
        VEC.reciprocal(self.recips[:, 8:8 + NJ], self.sumrow[:, 0:NJ])
        VEC.drain()
        self.wait(VEC, s_act, t_vtm)
        for j in range(NJ):
            VEC.scalar_tensor_tensor(
                self.negdel[:, j * 128:(j + 1) * 128],
                ps(3072 + j * 128, 3072 + (j + 1) * 128),
                self.recips[:, 4 + j:5 + j],
                self.v_tm[:, (seg * NJ + j) * 128: (seg * NJ + j + 1) * 128],
                ALU.mult, ALU.subtract)
        t_nd = self.vflush()
        self.bank_read([6], s_dve, t_nd)
        self.bank_read([7], s_dve, t_nd)

        self.pe_claim([7])
        self.wait(TEN, s_dve, t_nd)
        self.wait(TEN, s_act, t_sktm)
        ins = None
        for j in range(NJ):
            ins = TEN.matmul(ps(3712, 3840),
                             self.sk_tm[:, (seg * NJ + j) * 128:
                                        (seg * NJ + j + 1) * 128],
                             self.negdel[:, j * 128:(j + 1) * 128],
                             start=(j == 0), stop=(j == NJ - 1))
        t_upd = self.inc(ins, s_pe)

        pt_atok = 0
        for cpair in range(2):
            srcs = [self.p_sb[:, j * 512 + c * 128: j * 512 + (c + 1) * 128]
                    for c in (2 * cpair, 2 * cpair + 1) for j in range(NJ)]
            dst = self.pt_sb[:, cpair * 1024:(cpair + 1) * 1024]
            pt_ptok, pt_atok = self.transpose_batch(
                srcs, 0 if cpair == 0 else 512, dst_ap=dst,
                tok_req=(s_dve, t_pm))
        self.psb_lastuse_pe = pt_ptok
        self.pe_claim([4])
        self.wait(TEN, s_act, pt_atok)
        ins = None
        for j in range(NJ):
            for c in range(NJ):
                ins = TEN.matmul(ps(2048 + j * 128, 2048 + (j + 1) * 128),
                                 self.pt_sb[:, (c * NJ + j) * 128:
                                            (c * NJ + j + 1) * 128],
                                 self.v_tm[:, (seg * NJ + c) * 128:
                                           (seg * NJ + c + 1) * 128],
                                 start=(c == 0), stop=(c == NJ - 1))
        t_dot = self.inc(ins, s_pe)
        if self.fill_queue:
            self.fill_queue.pop(0)()

        self.wait(VEC, s_pe, t_upd)
        VEC.tensor_sub(self.mem_sb[:, h * 128:(h + 1) * 128],
                       self.mem_sb[:, h * 128:(h + 1) * 128],
                       ps(3712, 3840))
        VEC.reduce_sum(self.zsum[:], self.sk_fm[:, sbase:sbase + 512],
                       axis=AX)
        VEC.drain()
        VEC.tensor_add(self.z_sb[:, h:h + 1], self.z_sb[:, h:h + 1],
                       self.zsum[:])
        VEC.drain()
        VEC.tensor_copy(self.z_bf[:, h:h + 1], self.z_sb[:, h:h + 1])
        t_z = self.vflush()
        self.memupd_tok[h] = t_z
        self.bank_read([7], s_dve, t_z)

        self.wait(VEC, s_pe, t_dot)
        self.wait(VEC, s_pe, self.tail_trans_tok)
        for j in range(NJ):
            VEC.tensor_scalar_mul(self.adot_tm[:, j * 128:(j + 1) * 128],
                                  ps(2048 + j * 128, 2048 + (j + 1) * 128),
                                  self.recips[:, 8 + j:9 + j])
        VEC.drain()
        for j in range(NJ):
            VEC.scalar_tensor_tensor(
                self.dmd_tm[:, j * 128:(j + 1) * 128],
                ps(2560 + j * 128, 2560 + (j + 1) * 128),
                self.recips[:, 0 + j:1 + j],
                self.adot_tm[:, j * 128:(j + 1) * 128],
                ALU.mult, ALU.subtract)
        t_post = self.vflush()
        self.bank_read([4], s_dve, t_post)
        self.bank_read([5], s_dve, t_post)

        def tail():
            _, t_adfm = self.transpose_batch(
                [self.adot_tm[:, j * 128:(j + 1) * 128] for j in range(NJ)],
                2048, dst_ap=self.adot_fm[:, 0:512],
                tok_req=(s_dve, t_post))
            t_dmdp, _ = self.transpose_batch(
                [self.dmd_tm[:, j * 128:(j + 1) * 128] for j in range(NJ)],
                2304, tok_req=(s_dve, t_post))
            self.tail_trans_tok = t_dmdp
            self.wait(VEC, s_pe, t_dmdp)
            self.wait(VEC, s_act, t_adfm)
            self.wait(VEC, s_act, self.t_beta_sig)
            self.wait(VEC, s_st, st_hi)
            H = self.cfg.H
            U = 128 // H
            blkw = 512 // H
            dmd_fm = self.ps_bf(2304, 512)
            in0 = dmd_fm.rearrange("p (j u w) -> p w j u", j=NJ, u=U, w=H)
            in1 = self.adot_fm.ap().rearrange("p (j u w) -> p w j u", j=NJ,
                                              u=U, w=H)
            outv = self.att_all.rearrange("p (w x) -> p w x", w=H) \
                [:, :, sbase + h * blkw: sbase + (h + 1) * blkw] \
                .rearrange("p w (j u) -> p w j u", j=NJ)
            t_att = self.inc(VEC.scalar_tensor_tensor(
                outv, in0, self.beta_sb[:, h:h + 1], in1, ALU.mult, ALU.add),
                s_dve)
            self.att_tok_last = t_att
            self.bank_read([4], s_dve, t_att)
        return tail

    def build(self):
        self.declare_io()
        self.declare_mem()
        blk = self.nc.Block()
        blk.__enter__()

        @blk.gpsimd
        def _(g):
            self.emit()
        blk.__exit__(None, None, None)
        return self.nc


def _rearrange_weights(cfg, inp):
    NK, H, E = cfg.NK, cfg.H, cfg.E
    FEC, FSHC, NDQ = cfg.FEC, cfg.FSHC, cfg.NDQ

    def colblocks(w):
        m = w.shape[1] // 128
        return np.ascontiguousarray(
            w.reshape(NK, 128, m, 128).transpose(2, 1, 0, 3)
            .reshape(m, 128, NK * 128))

    out = {}
    out["wq_r"] = colblocks(inp["L0_attn_Wq"])
    out["wk_r"] = colblocks(inp["L0_attn_Wk"])
    out["wv_r"] = colblocks(inp["L0_attn_Wv"])
    out["wo_r"] = colblocks(inp["L0_attn_Wo"])
    out["betas_r"] = np.ascontiguousarray(
        np.asarray(inp["L0_attn_betas"]).reshape(H, 128).T)
    out["gate_r"] = np.ascontiguousarray(
        inp["L0_ffn_gate_w"].reshape(NK, 128, E).transpose(1, 0, 2)
        .reshape(128, NK * E))
    out["wg_r"] = np.ascontiguousarray(
        inp["L0_ffn_wg"].reshape(E, NK, 128, FEC, 128)
        .transpose(0, 3, 2, 1, 4).reshape(E, FEC, 128, NK * 128))
    out["wu_r"] = np.ascontiguousarray(
        inp["L0_ffn_wu"].reshape(E, NK, 128, FEC, 128)
        .transpose(0, 3, 2, 1, 4).reshape(E, FEC, 128, NK * 128))
    out["wd_r"] = np.ascontiguousarray(
        inp["L0_ffn_wd"].reshape(E, FEC, 128, NDQ, 512)
        .transpose(0, 3, 2, 1, 4).reshape(E, NDQ, 128, FEC * 512))
    out["sg_r"] = np.ascontiguousarray(
        inp["L0_ffn_sg"].reshape(NK, 128, FSHC, 128).transpose(2, 1, 0, 3)
        .reshape(FSHC, 128, NK * 128))
    out["su_r"] = np.ascontiguousarray(
        inp["L0_ffn_su"].reshape(NK, 128, FSHC, 128).transpose(2, 1, 0, 3)
        .reshape(FSHC, 128, NK * 128))
    out["sd_r"] = np.ascontiguousarray(
        inp["L0_ffn_sd"].reshape(FSHC, 128, NDQ, 512).transpose(2, 1, 0, 3)
        .reshape(NDQ, 128, FSHC * 512))
    return out


def kernel(**inputs):
    cfg = Cfg()
    inputs = {k: np.asarray(v) for k, v in inputs.items()}
    x = np.ascontiguousarray(inputs["L0_attn_x"])
    w = _rearrange_weights(cfg, inputs)
    nc = Builder(cfg).build()
    in_maps = []
    for c in range(8):
        m = dict(w)
        m["x"] = np.ascontiguousarray(x[c * cfg.B:(c + 1) * cfg.B])
        in_maps.append(m)
    res = run_bass_kernel_spmd(nc, in_maps, core_ids=list(range(8)))
    return np.concatenate([r["y"] for r in res.results], axis=0)
